# version 04
# baseline (speedup 1.0000x reference)
import sys
sys.path.insert(0, '/opt/trn_rl_repo')
import math
from contextlib import ExitStack

import numpy as np

import concourse.bass as bass
import concourse.mybir as mybir
from concourse.bass_core_rust import IndirectOffsetOnAxis
from concourse.bass_utils import run_bass_kernel_spmd

F32 = mybir.dt.float32
BF16 = mybir.dt.bfloat16
I32 = mybir.dt.int32
AF = mybir.ActivationFunctionType
ALU = mybir.AluOpType

B_FULL, S_FULL, D_FULL, H_FULL = 4, 4096, 2048, 16
HD = 128
NCORES = 8
EPS = 1e-5
CHKT = 8


def _chunks_for(NT):
    out = []
    for i in range(NT):
        for j0t in range(0, i + 1, CHKT):
            nj = min(CHKT, i + 1 - j0t)
            out.append((i, j0t, nj, j0t + nj - 1 == i))
    return out


class Prog:

    def __init__(self):
        self.q = {k: [] for k in ("sync", "gpsimd", "tensor", "vector", "scalar")}
        self.cnt = {}
        self.queue_sems = {"sync": set(), "gpsimd": set()}

    def inc(self, sem, k):
        self.cnt[sem] = self.cnt.get(sem, 0) + k
        return self.cnt[sem]

    def val(self, sem):
        return self.cnt.get(sem, 0)

    def emit(self, eng, fn):
        self.q[eng].append(fn)


def build_nc(S, D, HL, n_ag_cores, debug=False, phases=6, sub3=99):
    NT = S // 128
    KD = D // 128
    TC = S // 512
    DH2 = D // 2
    MH = DH2 // 128
    SCT = 1.0 / (30.0 * math.sqrt(HD))
    CH = _chunks_for(NT)
    NCH = len(CH)

    nc = bass.Bass("TRN2", target_bir_lowering=False, debug=debug)

    XT = nc.dram_tensor("XT", [D, S], F32, kind="ExternalInput").ap()
    WQ = nc.dram_tensor("WQ", [HL, D, 128], F32, kind="ExternalInput").ap()
    WK = nc.dram_tensor("WK", [HL, D, 128], F32, kind="ExternalInput").ap()
    WV = nc.dram_tensor("WV", [HL, D, 128], F32, kind="ExternalInput").ap()
    BQ = nc.dram_tensor("BQ", [1, HL * 128], F32, kind="ExternalInput").ap()
    BK = nc.dram_tensor("BK", [1, HL * 128], F32, kind="ExternalInput").ap()
    BV = nc.dram_tensor("BV", [1, HL * 128], F32, kind="ExternalInput").ap()
    SLP = nc.dram_tensor("SLP", [1, HL], F32, kind="ExternalInput").ap()
    WO = nc.dram_tensor("WO", [MH, D, 128], F32, kind="ExternalInput").ap()
    BO = nc.dram_tensor("BO", [1, DH2], F32, kind="ExternalInput").ap()
    WFN = nc.dram_tensor("WFN", [DH2, D], F32, kind="ExternalInput").ap()
    BF2 = nc.dram_tensor("BF2", [1, DH2], F32, kind="ExternalInput").ap()
    XRES = nc.dram_tensor("XRES", [DH2, S], F32, kind="ExternalInput").ap()
    CIDX = nc.dram_tensor("CIDX", [128, KD], I32, kind="ExternalInput").ap()
    JPC = nc.dram_tensor("JPC", [128, CHKT * 128], F32, kind="ExternalInput").ap()
    IVEC = nc.dram_tensor("IVEC", [1, NT], F32, kind="ExternalInput").ap()
    TRIU = nc.dram_tensor("TRIU", [128, 128], F32, kind="ExternalInput").ap()
    MSKD = nc.dram_tensor("MSKD", [128, 128], F32, kind="ExternalInput").ap()
    IDN = nc.dram_tensor("IDN", [128, 128], F32, kind="ExternalInput").ap()
    ONESR = nc.dram_tensor("ONESR", [1, 128], F32, kind="ExternalInput").ap()
    ONESQ = nc.dram_tensor("ONESQ", [128, 128], F32, kind="ExternalInput").ap()
    ONES2 = nc.dram_tensor("ONES2", [2, 128], F32, kind="ExternalInput").ap()
    ONESC = nc.dram_tensor("ONESC", [128, 1], F32, kind="ExternalInput").ap()
    ONE1 = nc.dram_tensor("ONE1", [1, 1], F32, kind="ExternalInput").ap()

    OUT = nc.dram_tensor("OUT", [DH2, S], F32, kind="ExternalOutput").ap()

    xnO = nc.dram_tensor("xnO", [D, S], BF16)
    ag_in = nc.dram_tensor("ag_in", [HL * 128, S], BF16)
    if n_ag_cores > 4:
        ag_out = nc.dram_tensor("ag_out", [n_ag_cores * HL * 128, S], BF16,
                                addr_space="Shared")
    else:
        ag_out = nc.dram_tensor("ag_out", [n_ag_cores * HL * 128, S], BF16)
    ctxF = nc.dram_tensor("ctxF", [D, S], BF16)

    P = Prog()

    with ExitStack() as stack:
        def sb(n, s, d):
            return stack.enter_context(nc.sbuf_tensor(n, s, d))

        def psum(n, s, d):
            return stack.enter_context(nc.psum_tensor(n, s, d))

        SEM_NAMES = ["qS", "qG", "qO", "cc",
                     "tLN", "tPRJ", "tTRV", "tMM1", "tMM2", "tTRC", "tTHR", "tOUT",
                     "vLN", "vEV", "vBias", "vCtx", "vCtxT", "vOut", "vTERN", "vMisc",
                     "vSelf", "sSelf",
                     "sTanh", "sExp", "sAbs", "bar",
                     "qGC", "qSTK", "qLa", "qLb", "qMa", "qMb",
                     "qHa", "qHb", "qAa", "qAb", "qWa", "qWb", "qWO",
                     "q3a", "q3b", "qGXa", "qGXb", "qSGa", "qSGb", "qOa", "qOb",
                     "qXa", "qXb"]
        sems = {n: stack.enter_context(nc.semaphore(n)) for n in SEM_NAMES}

        jp = sb("jp", [128, CHKT * 128], F32)
        t1 = sb("t1", [128, CHKT * 128], F32)
        c_all = sb("c_all", [128, HL * NT], F32)
        slope_bc = sb("slope_bc", [128, HL], F32)
        bqT = sb("bqT", [128, HL], F32)
        bkT = sb("bkT", [128, HL], F32)
        bvT = sb("bvT", [128, HL], F32)
        bias_sum = sb("bias_sum", [128, MH], F32)
        triu_bf = sb("triu_bf", [128, 128], BF16)
        mskneg = sb("mskneg", [128, 128], F32)
        tdg = sb("tdg", [128, 128], F32)
        idn_bf = sb("idn_bf", [128, 128], BF16)
        onesr_f = sb("onesr_f", [1, 128], F32)
        onesc_f = sb("onesc_f", [128, 1], F32)
        onesc_bf = sb("onesc_bf", [128, 1], BF16)
        one1_f = sb("one1_f", [1, 1], F32)
        one1_bf = sb("one1_bf", [1, 1], BF16)
        onesr_bf = sb("onesr_bf", [1, 128], BF16)
        onesq_bf = sb("onesq_bf", [128, 128], BF16)
        ones2_bf = sb("ones2_bf", [2, 128], BF16)
        cidx_sb = sb("cidx_sb", [128, KD], I32)
        rvbuf = sb("rvbuf", [128, 1], F32)
        lnrows_bf = sb("lnrows_bf", [1, 2048], BF16)

        ARENA_BYTES = 180 * 1024
        arena = sb("arena", [128, ARENA_BYTES], mybir.dt.uint8)

        def view(off, dt, cols):
            nb = cols * (4 if dt in (F32, I32) else 2)
            assert off % 4 == 0 and off + nb <= ARENA_BYTES, (off, nb)
            return arena[:, off:off + nb].bitcast(dt)

        def mkalloc():
            state = {"o": 0}
            def nxt(dt, cols):
                v = view(state["o"], dt, cols)
                state["o"] += cols * (4 if dt in (F32, I32) else 2)
                state["o"] = (state["o"] + 3) & ~3
                return v
            return nxt, state

        a, st_ = mkalloc()
        slp_row = a(F32, HL)[0:1, :]
        ivec_row = a(F32, NT)[0:1, :]
        bq_row = a(F32, HL * 128)[0:1, :]
        bk_row = a(F32, HL * 128)[0:1, :]
        bv_row = a(F32, HL * 128)[0:1, :]
        bo_row = a(F32, DH2)[0:1, :]
        bf_row = a(F32, DH2)[0:1, :]
        slp_bf = a(BF16, 2 * HL)[0:1, :]
        ivec_bf = a(BF16, NT)[0:1, :]
        brow_bf = a(BF16, 2 * 3 * HL * 128)[0:1, :]
        bsum_row = a(F32, DH2)[0:1, :]
        bsum_bf = a(BF16, 2 * DH2)[0:1, :]
        tmp_row = a(F32, max(DH2, HL * 128))[0:1, :]
        slp2 = a(BF16, 2 * HL)[0:2, 0:HL]
        brow2 = a(BF16, 2 * 3 * HL * 128)[0:2, 0:3 * HL * 128]
        bsum2 = a(BF16, 2 * DH2)[0:2, 0:DH2]

        a, st_ = mkalloc()
        ln_xc = [a(F32, KD * 512), a(F32, KD * 512)]
        ln_xcb = a(BF16, KD * 512)
        ln_xsq = a(BF16, KD * 512)
        ln_xw = a(F32, KD * 512)
        ln_xn = a(BF16, KD * 512)
        mbc = a(F32, 512)
        vbc = a(F32, 512)
        tbc = a(F32, 512)
        assert st_["o"] <= ARENA_BYTES

        a, st_ = mkalloc()
        qbuf = [a(BF16, NT * 128), a(BF16, NT * 128)]
        kbuf = [a(BF16, NT * 128), a(BF16, NT * 128)]
        vbuf = [a(BF16, NT * 129), a(BF16, NT * 129)]
        wqb = [a(BF16, KD * 128), a(BF16, KD * 128)]
        wkb = [a(BF16, KD * 128), a(BF16, KD * 128)]
        wvb = [a(BF16, KD * 128), a(BF16, KD * 128)]
        xslab = [a(BF16, KD * 512), a(BF16, KD * 512)]
        vt_tmp = a(BF16, 512)
        sc_sb = [a(F32, CHKT * 128), a(F32, CHKT * 128)]
        tanh_b = a(F32, CHKT * 128)
        probs = [a(BF16, CHKT * 128), a(BF16, CHKT * 128)]
        ctx_nat = a(BF16, NT * 128)
        ctxT_b = a(BF16, NT * 128)
        assert st_["o"] <= ARENA_BYTES, st_["o"]

        a, st_ = mkalloc()
        qt_sb = a(BF16, KD * DH2)
        wo_sb = a(BF16, MH * KD * 128)
        P3_SLAB0 = st_["o"]
        p3_xn = [a(BF16, KD * 512), a(BF16, KD * 512)]
        p3_ctx = [a(BF16, KD * 512), a(BF16, KD * 512)]
        P3_XR0 = st_["o"]
        p3_xr = [a(F32, MH * 512), a(F32, MH * 512)]
        out_st = [a(F32, 512), a(F32, 512)]
        assert st_["o"] <= ARENA_BYTES, st_["o"]

        o3 = P3_SLAB0
        def t3(dt, cols):
            nonlocal o3
            v = view(o3, dt, cols)
            o3 += cols * (4 if dt in (F32, I32) else 2)
            o3 = (o3 + 3) & ~3
            return v
        wfn = [t3(F32, D), t3(F32, D)]
        absn = [t3(F32, D), t3(F32, D)]
        qnat = [t3(BF16, D), t3(BF16, D)]
        m1b = t3(F32, D)
        m2b = t3(F32, D)
        thr_col = t3(F32, 2 * MH)
        redcol = t3(F32, 1)
        GB0 = (max(o3, P3_XR0) + 3) & ~3
        gbuf = [view(GB0, BF16, S), view(GB0 + 2 * S, BF16, S)]
        assert GB0 + 4 * S <= ARENA_BYTES, (GB0, ARENA_BYTES)

        ps_sc2 = [psum("ps_sc0", [128, CHKT * 128], F32),
                  psum("ps_sc1", [128, CHKT * 128], F32)]
        ps_sc = ps_sc2[0]
        ps_q = psum("ps_q", [128, 512], F32)
        ps_k = psum("ps_k", [128, 512], F32)
        ps_v = psum("ps_v", [128, 512], F32)
        ps_tr = psum("ps_tr", [128, 1024], BF16)

        V_SEMS = ("vLN", "vEV", "vBias", "vCtx", "vCtxT", "vOut", "vTERN",
                  "vMisc", "vSelf")
        S_SEMS = ("sTanh", "sExp", "sAbs")
        T_SEMS = ("tLN", "tPRJ", "tTRV", "tMM1", "tMM2", "tTRC", "tTHR", "tOUT")

        def barrier():
            qsw = tuple((s, P.val(s))
                        for s in sorted(P.queue_sems["sync"] | {"qS", "qO"}))
            qgw = tuple((s, P.val(s))
                        for s in sorted(P.queue_sems["gpsimd"] | {"qG"}))
            vw = tuple((s, P.val(s)) for s in V_SEMS)
            sw = tuple((s, P.val(s)) for s in S_SEMS)
            tw = tuple((s, P.val(s)) for s in T_SEMS)
            tgt = P.val("bar") + 5
            P.inc("bar", 5)
            def arm(ws):
                def f(e, ws=ws, t=tgt):
                    for (s, v) in ws:
                        e.wait_ge(sems[s], v)
                    e.nop().then_inc(sems["bar"], 1)
                    e.wait_ge(sems["bar"], t)
                return f
            P.emit("sync", arm(qsw))
            P.emit("gpsimd", arm(qgw))
            P.emit("tensor", arm(tw))
            P.emit("vector", arm(vw))
            P.emit("scalar", arm(sw))

        def dma(eng_q, sem_name, out_ap, in_ap, waits=()):
            P.queue_sems[eng_q].add(sem_name)
            tgt = P.inc(sem_name, 16)
            def fn(e, o=out_ap, i=in_ap, ws=tuple(waits)):
                for (s, v) in ws:
                    e.wait_ge(sems[s], v)
                e.dma_start(out=o, in_=i).then_inc(sems[sem_name], 16)
            P.emit(eng_q, fn)
            return tgt

        for (dst, src) in [(jp[:], JPC), (ivec_row[:], IVEC), (slp_row[:], SLP),
                           (bq_row[:], BQ), (bk_row[:], BK), (bv_row[:], BV),
                           (bo_row[:], BO), (bf_row[:], BF2), (cidx_sb[:], CIDX),
                           (onesr_f[:], ONESR), (onesc_f[:], ONESC), (one1_f[:], ONE1),
                           (mskneg[:], MSKD)]:
            dma("sync", "qS", dst, src)
        for (dst, src) in [(triu_bf[:], TRIU), (idn_bf[:], IDN), (onesc_bf[:], ONESC),
                           (onesr_bf[:], ONESR), (one1_bf[:], ONE1),
                           (onesq_bf[:], ONESQ), (ones2_bf[:], ONES2),
                           (ivec_bf, IVEC)]:
            dma("gpsimd", "qGC", dst, src)
        qs_consts = P.val("qS")
        qg_consts = P.val("qGC")

        vs0 = P.val("vSelf")
        P.inc("vSelf", 19)
        def consts_split(v, qs=qs_consts, qg=qg_consts, vs=vs0):
            I = sems["vSelf"]
            v.wait_ge(sems["qS"], qs)
            v.wait_ge(sems["qGC"], qg)
            v.tensor_copy(slp_bf[0:1, 0:HL], slp_row[:]).then_inc(I, 1)
            v.tensor_copy(brow_bf[0:1, 0:HL * 128], bq_row[:]).then_inc(I, 1)
            v.tensor_copy(brow_bf[0:1, HL * 128:2 * HL * 128], bk_row[:]).then_inc(I, 1)
            v.tensor_copy(brow_bf[0:1, 2 * HL * 128:3 * HL * 128],
                          bv_row[:]).then_inc(I, 1)
            v.tensor_tensor(bsum_row, bo_row[:], bf_row[:], ALU.add).then_inc(I, 1)
            v.wait_ge(I, vs + 5)
            def sub_hi(dst_row, bf_slice, n):
                v.tensor_copy(tmp_row[0:1, 0:n], bf_slice).then_inc(I, 1)
                v.wait_ge(I, P2_c[0] + 1)
                P2_c[0] += 2
                v.tensor_tensor(dst_row, dst_row, tmp_row[0:1, 0:n],
                                ALU.subtract).then_inc(I, 1)
            P2_c = [vs + 5]
            sub_hi(slp_row[:], slp_bf[0:1, 0:HL], HL)
            sub_hi(bq_row[:], brow_bf[0:1, 0:HL * 128], HL * 128)
            sub_hi(bk_row[:], brow_bf[0:1, HL * 128:2 * HL * 128], HL * 128)
            sub_hi(bv_row[:], brow_bf[0:1, 2 * HL * 128:3 * HL * 128], HL * 128)
            v.tensor_copy(bsum_bf[0:1, 0:DH2], bsum_row).then_inc(I, 1)
            v.wait_ge(I, vs + 14)
            v.tensor_copy(slp_bf[0:1, HL:2 * HL], slp_row[:]).then_inc(I, 1)
            v.tensor_copy(brow_bf[0:1, 3 * HL * 128:4 * HL * 128], bq_row[:]).then_inc(I, 1)
            v.tensor_copy(brow_bf[0:1, 4 * HL * 128:5 * HL * 128], bk_row[:]).then_inc(I, 1)
            v.tensor_copy(brow_bf[0:1, 5 * HL * 128:6 * HL * 128], bv_row[:]).then_inc(I, 1)
            v.tensor_copy(tmp_row[0:1, 0:DH2], bsum_bf[0:1, 0:DH2]).then_inc(I, 1)
            v.wait_ge(I, vs + 19)
            v.tensor_tensor(bsum_row, bsum_row, tmp_row[0:1, 0:DH2],
                            ALU.subtract).then_inc(sems["vMisc"], 1)
        P.inc("vMisc", 1)
        vm_sp = P.val("vMisc")
        P.inc("vMisc", 1)
        def consts_split2(v, vm=vm_sp):
            v.wait_ge(sems["vMisc"], vm)
            v.tensor_copy(bsum_bf[0:1, DH2:2 * DH2], bsum_row).then_inc(sems["vMisc"], 1)
        def consts_split_both(v):
            consts_split(v)
            consts_split2(v)
        P.emit("vector", consts_split_both)

        v_split = P.val("vMisc")
        dma("sync", "qSTK", slp2, slp_bf[0:1, 0:2 * HL], waits=[("vMisc", v_split)])
        dma("sync", "qSTK", brow2, brow_bf[0:1, 0:6 * HL * 128], waits=[])
        dma("sync", "qSTK", bsum2, bsum_bf[0:1, 0:2 * DH2], waits=[])
        qs_stack = P.val("qSTK")

        def consts_build(t, qs=qs_stack):
            t.wait_ge(sems["qSTK"], qs)
            t.matmul(ps_sc[:, 0:HL], ones2_bf[:], slp2, start=True, stop=True)
            t.matmul(ps_sc[:, 512:512 + NT], onesr_bf[:], ivec_bf,
                     start=True, stop=True)
            for h in range(HL):
                for (col, off) in ((h, 0), (HL + h, HL * 128), (2 * HL + h, 2 * HL * 128)):
                    t.matmul(ps_q[:, col:col + 1],
                             brow2[0:2, off + h * 128:off + (h + 1) * 128]
                             .transpose() if False else
                             brow2[0:2, off + h * 128:off + (h + 1) * 128],
                             ones2_bf[0:2, 0:1], start=True, stop=True)
            for m in range(MH):
                inst = t.matmul(ps_k[:, m:m + 1], bsum2[0:2, m * 128:(m + 1) * 128],
                                ones2_bf[0:2, 0:1], start=True, stop=True)
                if m == MH - 1:
                    inst.then_inc(sems["tLN"], 1)
        P.inc("tLN", 1)
        P.emit("tensor", consts_build)

        vs1 = P.val("vSelf")
        P.inc("vSelf", 6 + HL)
        def consts_evac(v, vs=vs1):
            I = sems["vSelf"]
            v.wait_ge(sems["tLN"], 1)
            v.tensor_copy(slope_bc[:], ps_sc[:, 0:HL]).then_inc(I, 1)
            v.tensor_copy(bqT[:], ps_q[:, 0:HL]).then_inc(I, 1)
            v.tensor_copy(bkT[:], ps_q[:, HL:2 * HL]).then_inc(I, 1)
            v.tensor_copy(bvT[:], ps_q[:, 2 * HL:3 * HL]).then_inc(I, 1)
            v.tensor_copy(bias_sum[:], ps_k[:, 0:MH]).then_inc(I, 1)
            v.tensor_copy(t1[:, 0:NT], ps_sc[:, 512:512 + NT]).then_inc(I, 1)
            v.wait_ge(I, vs + 6)
            for h in range(HL):
                v.tensor_scalar(c_all[:, h * NT:(h + 1) * NT], t1[:, 0:NT],
                                slope_bc[:, h:h + 1], None, ALU.mult).then_inc(I, 1)
            v.nop().then_inc(sems["vMisc"], 1)
        P.inc("vMisc", 1)
        P.emit("vector", consts_evac)

        barrier()

        xt_t = XT.rearrange("(k p) t -> p k t", p=128)
        xno_t = xnO.ap().rearrange("(k p) t -> p k t", p=128)
        VLN_PER_CHUNK = 4
        for c in range(TC if phases >= 2 else 0):
            w = [("vLN", VLN_PER_CHUNK * (c - 1))] if c >= 2 else []
            qL = "qLa" if c % 2 == 0 else "qLb"
            dma("sync", qL, ln_xc[c % 2][:].rearrange("p (k t) -> p k t", t=512),
                xt_t[:, :, c * 512:(c + 1) * 512], waits=w)
            qs_x = (qL, 16 * (c // 2 + 1))

            def ln_pre(v, c=c, qs=qs_x):
                v.wait_ge(sems[qs[0]], qs[1])
                v.tensor_copy(ln_xcb[:], ln_xc[c % 2][:]).then_inc(sems["vLN"], 1)
            P.inc("vLN", 1)
            P.emit("vector", ln_pre)
            vln_cast = P.val("vLN")

            def ln_sq(s, c=c, qs=qs_x):
                s.wait_ge(sems[qs[0]], qs[1])
                s.activation(ln_xsq[:], ln_xc[c % 2][:], AF.Square).then_inc(sems["sTanh"], 1)
            P.inc("sTanh", 1)
            P.emit("scalar", ln_sq)
            s_sq = P.val("sTanh")

            def ln_mms(t, vc=vln_cast, sq=s_sq):
                t.wait_ge(sems["vLN"], vc)
                t.wait_ge(sems["sTanh"], sq)
                for k in range(KD):
                    t.matmul(ps_q[:, 0:512], onesq_bf[:],
                             ln_xcb[:, k * 512:(k + 1) * 512],
                             start=(k == 0), stop=(k == KD - 1))
                for k in range(KD):
                    inst = t.matmul(ps_k[:, 0:512], onesq_bf[:],
                                    ln_xsq[:, k * 512:(k + 1) * 512],
                                    start=(k == 0), stop=(k == KD - 1))
                    if k == KD - 1:
                        inst.then_inc(sems["tLN"], 1)
            P.inc("tLN", 1)
            P.emit("tensor", ln_mms)
            t_stats = P.val("tLN")

            vsr = P.inc("vSelf", 4)
            def ln_stats1(v, ts=t_stats, vs=vsr):
                v.wait_ge(sems["tLN"], ts)
                v.tensor_single_scalar(mbc[:], ps_q[:, 0:512], 1.0 / D,
                                       ALU.mult).then_inc(sems["vSelf"], 1)
                v.tensor_single_scalar(vbc[:], ps_k[:, 0:512], 1.0 / D,
                                       ALU.mult).then_inc(sems["vSelf"], 1)
                v.wait_ge(sems["vSelf"], vs - 2)
                v.tensor_tensor(tbc[:], mbc[:], mbc[:], ALU.mult).then_inc(
                    sems["vSelf"], 1)
                v.wait_ge(sems["vSelf"], vs - 1)
                v.tensor_tensor(vbc[:], vbc[:], tbc[:], ALU.subtract).then_inc(
                    sems["vSelf"], 1)
                v.wait_ge(sems["vSelf"], vs)
                v.nop().then_inc(sems["vLN"], 1)
            P.inc("vLN", 1)
            P.emit("vector", ln_stats1)
            v_var = P.val("vLN")

            def ln_sqrt(s, vv=v_var):
                s.wait_ge(sems["vLN"], vv)
                s.activation(tbc[:], vbc[:], AF.Sqrt,
                             bias=np.float32(EPS)).then_inc(sems["sTanh"], 1)
            P.inc("sTanh", 1)
            P.emit("scalar", ln_sqrt)
            s_sqrt = P.val("sTanh")

            vsr2 = P.inc("vSelf", 2)
            def ln_stats2(v, ss=s_sqrt, vs=vsr2):
                v.wait_ge(sems["sTanh"], ss)
                v.reciprocal(vbc[:], tbc[:]).then_inc(sems["vSelf"], 1)
                v.wait_ge(sems["vSelf"], vs - 1)
                v.tensor_tensor(tbc[:], mbc[:], vbc[:], ALU.mult).then_inc(
                    sems["vSelf"], 1)
                v.wait_ge(sems["vSelf"], vs)
                v.nop().then_inc(sems["vLN"], 1)
            P.inc("vLN", 1)
            P.emit("vector", ln_stats2)
            v_rs = P.val("vLN")

            qs_prev_spill = (("qMa" if (c - 1) % 2 == 0 else "qMb"),
                             16 * ((c - 1) // 2 + 1)) if c >= 1 else None

            vs_ap = P.inc("vSelf", 1)
            def ln_apply(v, c=c, vr=v_rs, qs=qs_prev_spill, vs=vs_ap):
                v.wait_ge(sems["vLN"], vr)
                if qs is not None:
                    v.wait_ge(sems[qs[0]], qs[1])
                rs_b = vbc[:].unsqueeze(1).broadcast_to([128, KD, 512])
                ms_b = tbc[:].unsqueeze(1).broadcast_to([128, KD, 512])
                xc3 = ln_xc[c % 2][:].rearrange("p (k t) -> p k t", t=512)
                xw3 = ln_xw[:].rearrange("p (k t) -> p k t", t=512)
                xn3 = ln_xn[:].rearrange("p (k t) -> p k t", t=512)
                v.tensor_tensor(xw3, xc3, rs_b, ALU.mult).then_inc(sems["vSelf"], 1)
                v.wait_ge(sems["vSelf"], vs)
                v.tensor_tensor(xn3, xw3, ms_b, ALU.subtract).then_inc(sems["vLN"], 1)
            P.inc("vLN", 1)
            P.emit("vector", ln_apply)
            v_app = P.val("vLN")

            dma("sync", "qMa" if c % 2 == 0 else "qMb",
                xno_t[:, :, c * 512:(c + 1) * 512],
                ln_xn[:].rearrange("p (k t) -> p k t", t=512), waits=[("vLN", v_app)])

        barrier()

        if phases < 3:
            P.emit("sync", lambda e, f=P.val("qS"): e.wait_ge(sems["qS"], f))
            return _finish(nc, P, sems)
        sTanh0 = P.val("sTanh")
        wq_all = WQ.rearrange("h (k p) c -> p (h k) c", p=128)
        wk_all = WK.rearrange("h (k p) c -> p (h k) c", p=128)
        wv_all = WV.rearrange("h (k p) c -> p (h k) c", p=128)

        qg_after_w = {}

        def w_dma(h):
            w = [("tPRJ", (h - 1) * TC)] if h >= 2 else []
            qH = "qHa" if h % 2 == 0 else "qHb"
            for (buf, src) in ((wqb[h % 2], wq_all), (wkb[h % 2], wk_all),
                               (wvb[h % 2], wv_all)):
                dma("gpsimd", qH, buf[:].rearrange("p (k c) -> p k c", c=128),
                    src[:, h * KD:(h + 1) * KD, :], waits=w)
            qg_after_w[h] = (qH, 48 * (h // 2 + 1))

        w_dma(0)
        if HL > 1:
            w_dma(1)

        for h in range(HL):
            t1_wait = sTanh0 + h * NCH
            if h == 0:
                P.inc("vSelf", 2)
            ones_done = P.val("vSelf")
            def t1_build(v, h=h, tw=t1_wait):
                if h == 0:
                    v.wait_ge(sems["vMisc"], 2)
                    v.memset(vbuf[0][:].rearrange("p (n c) -> p n c", c=129)[:, :, 128:129],
                             1.0).then_inc(sems["vSelf"], 1)
                    v.memset(vbuf[1][:].rearrange("p (n c) -> p n c", c=129)[:, :, 128:129],
                             1.0).then_inc(sems["vSelf"], 1)
                else:
                    v.wait_ge(sems["sTanh"], tw)
                v.tensor_scalar(t1[:], jp[:], slope_bc[:, h:h + 1], None,
                                ALU.mult).then_inc(sems["vMisc"], 1)
                v.scalar_tensor_tensor(tdg[:], jp[:, 0:128],
                                       slope_bc[:, h:h + 1], mskneg[:],
                                       ALU.mult, ALU.add).then_inc(sems["vMisc"], 1)
            P.inc("vMisc", 2)
            P.emit("vector", t1_build)

            for tc in range(TC):
                idx = h * TC + tc
                w = [("tPRJ", idx - 1)] if idx >= 2 else []
                qX = "qXa" if idx % 2 == 0 else "qXb"
                dma("sync", qX, xslab[tc % 2][:].rearrange("p (k t) -> p k t", t=512),
                    xno_t[:, :, tc * 512:(tc + 1) * 512], waits=w)
                qs_slab = (qX, 16 * (idx // 2 + 1))

                def prj(t, h=h, tc=tc, qs=qs_slab, qg=qg_after_w[h], s3=sub3):
                    t.wait_ge(sems[qs[0]], qs[1])
                    t.wait_ge(sems[qg[0]], qg[1])
                    if tc == 0 and h > 0 and s3 >= 3:
                        t.wait_ge(sems["vCtx"], h * NT)
                    if h * TC + tc >= 1:
                        t.wait_ge(sems["vEV"], h * TC + tc)
                    sl = xslab[tc % 2]
                    for k in range(KD):
                        st = (k == 0)
                        sp = (k == KD - 1)
                        t.matmul(ps_q[:, 0:512], wqb[h % 2][:, k * 128:(k + 1) * 128],
                                 sl[:, k * 512:(k + 1) * 512], start=st, stop=sp)
                        t.matmul(ps_k[:, 0:512], wkb[h % 2][:, k * 128:(k + 1) * 128],
                                 sl[:, k * 512:(k + 1) * 512], start=st, stop=sp)
                        inst = t.matmul(ps_v[:, 0:512],
                                        wvb[h % 2][:, k * 128:(k + 1) * 128],
                                        sl[:, k * 512:(k + 1) * 512], start=st, stop=sp)
                        if sp:
                            inst.then_inc(sems["tPRJ"], 1)
                P.inc("tPRJ", 1)
                P.emit("tensor", prj)
                t_prj = P.val("tPRJ")

                P.inc("vSelf", 2)
                def prj_ev(v, h=h, tc=tc, tp=t_prj, trv=P.val("tTRV")):
                    v.wait_ge(sems["tPRJ"], tp)
                    if h * TC + tc >= 1:
                        v.wait_ge(sems["tTRV"], trv)
                    v.tensor_scalar(qbuf[h % 2][:, tc * 512:(tc + 1) * 512],
                                    ps_q[:, 0:512], bqT[:, h:h + 1], None,
                                    ALU.add).then_inc(sems["vSelf"], 1)
                    v.tensor_scalar(kbuf[h % 2][:, tc * 512:(tc + 1) * 512],
                                    ps_k[:, 0:512], bkT[:, h:h + 1], None,
                                    ALU.add).then_inc(sems["vSelf"], 1)
                    v.tensor_scalar(vt_tmp[:], ps_v[:, 0:512], bvT[:, h:h + 1],
                                    None, ALU.add).then_inc(sems["vEV"], 1)
                P.inc("vEV", 1)
                P.emit("vector", prj_ev)
                v_ev = P.val("vEV")

                def vtr(t, ve=v_ev, vct=P.val("vCtxT")):
                    t.wait_ge(sems["vEV"], ve)
                    t.wait_ge(sems["vCtxT"], vct)
                    for q4 in range(4):
                        inst = t.transpose(ps_tr[:, q4 * 128:(q4 + 1) * 128],
                                           vt_tmp[:, q4 * 128:(q4 + 1) * 128], idn_bf[:])
                    inst.then_inc(sems["tTRV"], 1)
                P.inc("tTRV", 1)
                P.emit("tensor", vtr)
                t_trv = P.val("tTRV")

                def vev2(v, h=h, tc=tc, tt=t_trv):
                    v.wait_ge(sems["tTRV"], tt)
                    dst = vbuf[h % 2][:, tc * 4 * 129:(tc * 4 + 4) * 129] \
                        .rearrange("p (n c) -> p n c", c=129)[:, :, 0:128]
                    src = ps_tr[:, 0:512].rearrange("p (n c) -> p n c", c=128)
                    v.tensor_copy(dst, src).then_inc(sems["vCtxT"], 1)
                P.inc("vCtxT", 1)
                P.emit("vector", vev2)

            if h + 2 < HL:
                w_dma(h + 2)

            def emit_mm2(pd):
                def mm2(t, pd=pd):
                    t.wait_ge(sems["sExp"], pd["s_e"])
                    if pd["j0t"] == 0:
                        t.wait_ge(sems["vCtx"], pd["hh"] * NT + pd["i"])
                    for jl in range(pd["nj"]):
                        first = (pd["j0t"] == 0 and jl == 0)
                        last = (pd["isdiag"] and jl == pd["nj"] - 1)
                        inst = t.matmul(
                            ps_v[:, 0:129],
                            probs[pd["gci"] % 2][:, jl * 128:(jl + 1) * 128],
                            vbuf[pd["hh"] % 2][:, (pd["j0t"] + jl) * 129:
                                               (pd["j0t"] + jl + 1) * 129],
                            start=first, stop=last)
                        if jl == pd["nj"] - 1:
                            inst.then_inc(sems["tMM2"], 1)
                P.inc("tMM2", 1)
                P.emit("tensor", mm2)
                t_m2 = P.val("tMM2")

                if pd["isdiag"]:
                    vself = P.inc("vSelf", 1)
                    def vctx(v, i=pd["i"], tm2=t_m2, vs=vself):
                        v.wait_ge(sems["tMM2"], tm2)
                        v.reciprocal(rvbuf[:], ps_v[:, 128:129]).then_inc(sems["vSelf"], 1)
                        v.wait_ge(sems["vSelf"], vs)
                        v.tensor_scalar(ctx_nat[:, i * 128:(i + 1) * 128],
                                        ps_v[:, 0:128], rvbuf[:], None,
                                        ALU.mult).then_inc(sems["vCtx"], 1)
                    P.inc("vCtx", 1)
                    P.emit("vector", vctx)

            pend = None
            for ci, (i, j0t, nj, isdiag) in enumerate(CH):
                gci = h * NCH + ci

                def mm1(t, h=h, i=i, j0t=j0t, nj=nj, gci=gci, ci=ci,
                        vsn=P.val("vSelf"), vct=P.val("vCtxT")):
                    if ci == 0:
                        t.wait_ge(sems["vEV"], (h + 1) * TC)
                        t.wait_ge(sems["vSelf"], vsn)
                        t.wait_ge(sems["vCtxT"], vct)
                    if gci >= 2:
                        t.wait_ge(sems["vBias"], gci - 1)
                    for jl in range(nj):
                        inst = t.matmul(
                            ps_sc2[gci % 2][:, jl * 128:(jl + 1) * 128],
                            kbuf[h % 2][:, (j0t + jl) * 128:(j0t + jl + 1) * 128],
                            qbuf[h % 2][:, i * 128:(i + 1) * 128],
                            start=True, stop=True)
                        if jl == nj - 1:
                            inst.then_inc(sems["tMM1"], 1)
                P.inc("tMM1", 1)
                P.emit("tensor", mm1)
                t_mm1 = P.val("tMM1")

                def vbias(v, h=h, i=i, j0t=j0t, nj=nj, gci=gci, tm=t_mm1,
                          isdiag=isdiag, vmc=P.val("vMisc")):
                    v.wait_ge(sems["tMM1"], tm)
                    if gci % NCH == 0:
                        v.wait_ge(sems["vMisc"], vmc)
                    if gci >= 2:
                        v.wait_ge(sems["sTanh"], sTanh0 + gci - 1)
                    nr = nj - 1 if isdiag else nj
                    if nr > 0:
                        inst = v.scalar_tensor_tensor(
                            sc_sb[gci % 2][:, 0:nr * 128],
                            ps_sc2[gci % 2][:, 0:nr * 128],
                            c_all[:, h * NT + (i - j0t):h * NT + (i - j0t) + 1],
                            t1[:, 0:nr * 128], ALU.add, ALU.add)
                    if isdiag:
                        inst = v.tensor_tensor(
                            sc_sb[gci % 2][:, (nj - 1) * 128:nj * 128],
                            ps_sc2[gci % 2][:, (nj - 1) * 128:nj * 128],
                            tdg[:], ALU.add)
                    inst.then_inc(sems["vBias"], 1)
                P.inc("vBias", 1)
                P.emit("vector", vbias)
                v_b = P.val("vBias")

                def stanh(s, nj=nj, gci=gci, vb=v_b):
                    s.wait_ge(sems["vBias"], vb)
                    s.activation(tanh_b[:, 0:nj * 128], sc_sb[gci % 2][:, 0:nj * 128],
                                 AF.Tanh, scale=SCT).then_inc(sems["sTanh"], 1)
                P.inc("sTanh", 1)
                st_now = P.val("sTanh")
                P.emit("scalar", stanh)

                def sexp(s, nj=nj, gci=gci, ss=st_now):
                    if gci >= 2:
                        s.wait_ge(sems["tMM2"], gci - 1)
                    s.wait_ge(sems["sTanh"], ss)
                    s.activation(probs[gci % 2][:, 0:nj * 128], tanh_b[:, 0:nj * 128],
                                 AF.Exp, scale=30.0).then_inc(sems["sExp"], 1)
                P.inc("sExp", 1)
                P.emit("scalar", sexp)
                s_e = P.val("sExp")

                if pend is not None:
                    emit_mm2(pend)
                pend = dict(hh=h, i=i, j0t=j0t, nj=nj, isdiag=isdiag, gci=gci,
                            s_e=s_e)
            if pend is not None:
                emit_mm2(pend)
                pend = None

            if sub3 < 4:
                continue
            qs_prev_ctx_dma = (("qAa" if (h - 1) % 2 == 0 else "qAb"),
                               16 * ((h - 1) // 2 + 1)) if h >= 1 else None
            NG = (NT + 7) // 8
            for g in range(NG):
                n8 = min(8, NT - g * 8)
                def ctr(t, h=h, g=g, n8=n8, vct=P.val("vCtxT")):
                    t.wait_ge(sems["vCtx"], (h + 1) * NT)
                    t.wait_ge(sems["vCtxT"], vct)
                    for q8 in range(n8):
                        ti = g * 8 + q8
                        inst = t.transpose(ps_tr[:, q8 * 128:(q8 + 1) * 128],
                                           ctx_nat[:, ti * 128:(ti + 1) * 128], idn_bf[:])
                    inst.then_inc(sems["tTRC"], 1)
                P.inc("tTRC", 1)
                P.emit("tensor", ctr)
                t_trc = P.val("tTRC")

                def ctev(v, g=g, n8=n8, tt=t_trc, qs=qs_prev_ctx_dma):
                    v.wait_ge(sems["tTRC"], tt)
                    if g == 0 and qs is not None:
                        v.wait_ge(sems[qs[0]], qs[1])
                    v.tensor_copy(ctxT_b[:, g * 1024:g * 1024 + n8 * 128],
                                  ps_tr[:, 0:n8 * 128]).then_inc(sems["vCtxT"], 1)
                P.inc("vCtxT", 1)
                P.emit("vector", ctev)
            dma("sync", "qAa" if h % 2 == 0 else "qAb",
                ag_in.ap()[h * 128:(h + 1) * 128, :], ctxT_b[:],
                waits=[("vCtxT", P.val("vCtxT"))])

        if debug:
            DBG_Q = nc.dram_tensor("DBG_Q", [128, NT * 128], BF16, kind="ExternalOutput").ap()
            DBG_K = nc.dram_tensor("DBG_K", [128, NT * 128], BF16, kind="ExternalOutput").ap()
            DBG_V = nc.dram_tensor("DBG_V", [128, NT * 129], BF16, kind="ExternalOutput").ap()
            DBG_CN = nc.dram_tensor("DBG_CN", [128, NT * 128], BF16, kind="ExternalOutput").ap()
            dma("sync", "qS", DBG_Q, qbuf[0][:], waits=[("vCtxT", P.val("vCtxT"))])
            dma("sync", "qS", DBG_K, kbuf[0][:], waits=[])
            dma("sync", "qS", DBG_V, vbuf[0][:], waits=[])
            dma("sync", "qS", DBG_CN, ctx_nat[:], waits=[])

        qs_all_ctx = P.val("qS")
        barrier()

        if phases < 5:
            P.emit("gpsimd", lambda e, f=P.val("qG"): e.wait_ge(sems["qG"], f))
            return _finish(nc, P, sems)
        wfn_t = WFN.rearrange("(m p) c -> p m c", p=128)
        for m in range(MH):
            w = [("vTERN", P.val("vTERN") - 1)] if m >= 2 else []
            qW = "qWa" if m % 2 == 0 else "qWb"
            dma("sync", qW, wfn[m % 2][:].unsqueeze(1), wfn_t[:, m:m + 1, :], waits=w)
            qs_w = (qW, 16 * (m // 2 + 1))

            def sabs(s, m=m, qs=qs_w, vt=P.val("vTERN")):
                s.wait_ge(sems[qs[0]], qs[1])
                if m >= 2:
                    s.wait_ge(sems["vTERN"], vt)
                s.activation(absn[m % 2][:], wfn[m % 2][:], AF.Abs).then_inc(
                    sems["sAbs"], 1)
            P.inc("sAbs", 1)
            P.emit("scalar", sabs)
            s_a = P.val("sAbs")

            vst = P.inc("vSelf", 7)
            def tern_m(v, m=m, sa=s_a, vs=vst):
                I = sems["vSelf"]
                v.wait_ge(sems["sAbs"], sa)
                v.reduce_sum(redcol[:], absn[m % 2][:],
                             axis=mybir.AxisListType.X).then_inc(I, 1)
                v.wait_ge(I, vs - 6)
                v.tensor_single_scalar(thr_col[:, m:m + 1], redcol[:], 0.7 / D,
                                       ALU.mult).then_inc(I, 1)
                v.wait_ge(I, vs - 5)
                v.tensor_single_scalar(thr_col[:, MH + m:MH + m + 1],
                                       thr_col[:, m:m + 1], -1.0,
                                       ALU.mult).then_inc(I, 1)
                v.wait_ge(I, vs - 4)
                v.tensor_scalar(m1b[:], wfn[m % 2][:], thr_col[:, MH + m:MH + m + 1],
                                None, ALU.add).then_inc(I, 1)
                v.tensor_scalar(m2b[:], wfn[m % 2][:], thr_col[:, m:m + 1],
                                None, ALU.add).then_inc(I, 1)
                v.wait_ge(I, vs - 2)
                v.tensor_single_scalar(m1b[:], m1b[:], 0.0, ALU.is_gt).then_inc(I, 1)
                v.tensor_single_scalar(m2b[:], m2b[:], 0.0, ALU.is_lt).then_inc(I, 1)
                v.wait_ge(I, vs)
                v.tensor_tensor(qnat[m % 2][:], m1b[:], m2b[:],
                                ALU.subtract).then_inc(sems["vTERN"], 1)
            P.inc("vTERN", 1)
            P.emit("vector", tern_m)
            v_q = P.val("vTERN")

            NG8 = (KD + 7) // 8
            for g8 in range(NG8):
                n8 = min(8, KD - g8 * 8)
                def qtr(t, m=m, g8=g8, n8=n8, vq=v_q, vct=P.val("vCtxT")):
                    t.wait_ge(sems["vTERN"], vq)
                    t.wait_ge(sems["vCtxT"], vct)
                    for q8 in range(n8):
                        kk = g8 * 8 + q8
                        inst = t.transpose(ps_tr[:, q8 * 128:(q8 + 1) * 128],
                                           qnat[m % 2][:, kk * 128:(kk + 1) * 128],
                                           idn_bf[:])
                    inst.then_inc(sems["tTHR"], 1)
                P.inc("tTHR", 1)
                P.emit("tensor", qtr)
                t_q = P.val("tTHR")

                def qev(v, m=m, g8=g8, n8=n8, tq=t_q):
                    v.wait_ge(sems["tTHR"], tq)
                    dst = qt_sb[:, 0:KD * DH2].rearrange("p (k c) -> p k c", c=DH2) \
                        [:, g8 * 8:g8 * 8 + n8, m * 128:(m + 1) * 128]
                    src_ = ps_tr[:, 0:n8 * 128].rearrange("p (n c) -> p n c", c=128)
                    v.tensor_copy(dst, src_).then_inc(sems["vCtxT"], 1)
                P.inc("vCtxT", 1)
                P.emit("vector", qev)
        v_tern_done = P.val("vCtxT")

        wo_all = WO.rearrange("m (k p) c -> p (m k) c", p=128)
        for m in range(MH):
            dma("gpsimd", "qWO", wo_sb[:, m * KD * 128:(m + 1) * KD * 128]
                .rearrange("p (k c) -> p k c", c=128),
                wo_all[:, m * KD:(m + 1) * KD, :])
        qg_wo = P.val("qWO")

        if phases < 4:
            P.emit("sync", lambda e, f=P.val("qS"): e.wait_ge(sems["qS"], f))
            return _finish(nc, P, sems)
        def coll(g, qs=qs_all_ctx):
            g.wait_ge(sems["qS"], qs)
            g.collective_compute(
                "AllGather", ALU.bypass,
                replica_groups=[list(range(n_ag_cores))],
                ins=[ag_in.ap().opt()],
                outs=[ag_out.ap().opt()],
            ).then_inc(sems["cc"], 1)
            g.wait_ge(sems["cc"], 1)
        P.inc("cc", 1)
        P.emit("gpsimd", coll)

        ctxF_t = ctxF.ap().rearrange("(k p) t -> p k t", p=128)
        P.queue_sems["gpsimd"] |= {"qGXa", "qGXb", "qSGa", "qSGb"}
        for k in range(KD):
            qGX = "qGXa" if k % 2 == 0 else "qGXb"
            qSG = "qSGa" if k % 2 == 0 else "qSGb"
            ws = [(qSG, 16 * (k // 2))] if k >= 2 else []
            P.inc(qGX, 16)
            def gat(g, k=k, ws=tuple(ws), qGX=qGX):
                for (s, v) in ws:
                    g.wait_ge(sems[s], v)
                g.indirect_dma_start(
                    gbuf[k % 2][:], None,
                    ag_out.ap(), IndirectOffsetOnAxis(cidx_sb[:, k:k + 1], 0),
                ).then_inc(sems[qGX], 16)
            P.emit("gpsimd", gat)
            g_done = (qGX, 16 * (k // 2 + 1))
            P.inc(qSG, 16)
            def stg(g, k=k, gd=g_done, qSG=qSG):
                g.wait_ge(sems[gd[0]], gd[1])
                g.dma_start(out=ctxF_t[:, k:k + 1, :],
                            in_=gbuf[k % 2][:].unsqueeze(1)).then_inc(sems[qSG], 16)
            P.emit("gpsimd", stg)


        barrier()

        if phases < 6:
            P.emit("sync", lambda e, f=P.val("qS"): e.wait_ge(sems["qS"], f))
            P.emit("gpsimd", lambda e, f=P.val("qG"): e.wait_ge(sems["qG"], f))
            return _finish(nc, P, sems)
        xres_t = XRES.rearrange("(m p) t -> p m t", p=128)
        n_out = 0
        for tc in range(TC):
            w = [("tOUT", (tc - 1) * MH)] if tc >= 2 else []
            q3 = "q3a" if tc % 2 == 0 else "q3b"
            dma("sync", q3, p3_xn[tc % 2][:].rearrange("p (k t) -> p k t", t=512),
                xno_t[:, :, tc * 512:(tc + 1) * 512], waits=w)
            dma("sync", q3, p3_ctx[tc % 2][:].rearrange("p (k t) -> p k t", t=512),
                ctxF_t[:, :, tc * 512:(tc + 1) * 512])
            wv_ = [("vOut", (tc - 1) * MH)] if tc >= 2 else []
            dma("sync", q3, p3_xr[tc % 2][:].rearrange("p (m t) -> p m t", t=512),
                xres_t[:, :, tc * 512:(tc + 1) * 512], waits=wv_)
            qs_slabs = (q3, 48 * (tc // 2 + 1))

            for m in range(MH):
                ps_o = ps_q if m % 2 == 0 else ps_k
                n_out += 1

                def p3mm(t, tc=tc, m=m, qs=qs_slabs, ps_o=ps_o, n=n_out):
                    t.wait_ge(sems[qs[0]], qs[1])
                    if n == 1:
                        t.wait_ge(sems["vCtxT"], v_tern_done)
                        t.wait_ge(sems["qWO"], qg_wo)
                    if n >= 3:
                        t.wait_ge(sems["vOut"], n - 2)
                    for k in range(KD):
                        t.matmul(ps_o[:, 0:512],
                                 qt_sb[:, k * DH2 + m * 128:k * DH2 + (m + 1) * 128],
                                 p3_xn[tc % 2][:, k * 512:(k + 1) * 512],
                                 start=(k == 0), stop=False)
                    for k in range(KD):
                        inst = t.matmul(
                            ps_o[:, 0:512],
                            wo_sb[:, (m * KD + k) * 128:(m * KD + k + 1) * 128],
                            p3_ctx[tc % 2][:, k * 512:(k + 1) * 512],
                            start=False, stop=(k == KD - 1))
                        if k == KD - 1:
                            inst.then_inc(sems["tOUT"], 1)
                P.inc("tOUT", 1)
                P.emit("tensor", p3mm)
                t_o = P.val("tOUT")

                qOp = "qOa" if m % 2 == 0 else "qOb"
                st_guard = (qOp, P.val(qOp)) if P.val(qOp) > 0 else None

                def p3ev(v, tc=tc, m=m, to=t_o, ps_o=ps_o, sg=st_guard):
                    v.wait_ge(sems["tOUT"], to)
                    if sg is not None:
                        v.wait_ge(sems[sg[0]], sg[1])
                    v.scalar_tensor_tensor(out_st[m % 2][:], ps_o[:, 0:512],
                                           bias_sum[:, m:m + 1],
                                           p3_xr[tc % 2][:, m * 512:(m + 1) * 512],
                                           ALU.add, ALU.add).then_inc(sems["vOut"], 1)
                P.inc("vOut", 1)
                P.emit("vector", p3ev)
                v_o = P.val("vOut")

                P.queue_sems["sync"].add(qOp)
                P.inc(qOp, 16)
                def p3st(e, tc=tc, m=m, vo=v_o, qOp=qOp):
                    e.wait_ge(sems["vOut"], vo)
                    e.dma_start(out=OUT[m * 128:(m + 1) * 128, tc * 512:(tc + 1) * 512],
                                in_=out_st[m % 2][:]).then_inc(sems[qOp], 16)
                P.emit("sync", p3st)

        if debug:
            DBG_XNO = nc.dram_tensor("DBG_XNO", [D, S], BF16, kind="ExternalOutput").ap()
            DBG_AGI = nc.dram_tensor("DBG_AGI", [HL * 128, S], BF16, kind="ExternalOutput").ap()
            DBG_CF = nc.dram_tensor("DBG_CF", [D, S], BF16, kind="ExternalOutput").ap()
            DBG_QT = nc.dram_tensor("DBG_QT", [128, KD * DH2], BF16, kind="ExternalOutput").ap()
            DBG_THRB = nc.dram_tensor("DBG_THRB", [128, 2 * MH], F32, kind="ExternalOutput").ap()
            dma("sync", "qS", DBG_QT, qt_sb[:], waits=[("qO", P.val("qO"))])
            dma("sync", "qS", DBG_THRB, thr_col[:], waits=[])
            DBG_T1 = nc.dram_tensor("DBG_T1", [128, CHKT * 128], F32, kind="ExternalOutput").ap()
            DBG_CALL = nc.dram_tensor("DBG_CALL", [128, HL * NT], F32, kind="ExternalOutput").ap()
            DBG_SLB = nc.dram_tensor("DBG_SLB", [128, HL], F32, kind="ExternalOutput").ap()
            DBG_JP = nc.dram_tensor("DBG_JP", [128, CHKT * 128], F32, kind="ExternalOutput").ap()
            dma("sync", "qS", DBG_T1, t1[:], waits=[("qO", P.val("qO"))])
            dma("sync", "qS", DBG_CALL, c_all[:], waits=[])
            dma("sync", "qS", DBG_SLB, slope_bc[:], waits=[])
            dma("sync", "qS", DBG_JP, jp[:], waits=[])
            dma("sync", "qS", DBG_XNO, xnO.ap(), waits=[])
            dma("sync", "qS", DBG_AGI, ag_in.ap(), waits=[])
            dma("sync", "qS", DBG_CF, ctxF.ap(), waits=[])

        drain_s = tuple((s, P.val(s))
                        for s in sorted(P.queue_sems["sync"] | {"qS", "qO"}))
        out_tot = tuple((s, P.val(s)) for s in ("qOa", "qOb"))

        def fin_s(e, ws=drain_s):
            for (s, v) in ws:
                e.wait_ge(sems[s], v)
        P.emit("sync", fin_s)

        def fin_g(e, ws=out_tot):
            for (s, v) in ws:
                e.wait_ge(sems[s], v)
        P.emit("gpsimd", fin_g)

        return _finish(nc, P, sems)


def _finish(nc, P, sems):
    with nc.Block() as block:
        @block.sync
        def _(eng):
            for fn in P.q["sync"]:
                fn(eng)

        @block.gpsimd
        def _(eng):
            for fn in P.q["gpsimd"]:
                fn(eng)

        @block.tensor
        def _(eng):
            for fn in P.q["tensor"]:
                fn(eng)

        @block.vector
        def _(eng):
            for fn in P.q["vector"]:
                fn(eng)

        @block.scalar
        def _(eng):
            for fn in P.q["scalar"]:
                fn(eng)

    return nc


def make_consts(S):
    NT = S // 128
    g = np.arange(CHKT * 128)
    p = np.arange(128)
    JPC = (128 * (g[None, :] // 128) + p[:, None] - (g[None, :] % 128)).astype(np.float32)
    IVEC = (-128.0 * np.arange(NT, dtype=np.float32)).reshape(1, NT)
    TRIU = (np.arange(128)[None, :] >= np.arange(128)[:, None]).astype(np.float32)
    MSKD = np.where(np.arange(128)[:, None] > np.arange(128)[None, :],
                    np.float32(-1e6), np.float32(0.0)).astype(np.float32)
    IDN = np.eye(128, dtype=np.float32)
    return dict(JPC=JPC, IVEC=IVEC, TRIU=TRIU, MSKD=MSKD, IDN=IDN,
                ONESR=np.ones((1, 128), np.float32),
                ONESC=np.ones((128, 1), np.float32),
                ONESQ=np.ones((128, 128), np.float32),
                ONES2=np.ones((2, 128), np.float32),
                ONE1=np.ones((1, 1), np.float32))


def make_core_inputs(c, x, Wq, bq, Wk, bk, Wv, bv, Wo, bo, slopes, Wf, bf, S, D, HL):
    DH2 = D // 2
    MH = DH2 // 128
    KD = D // 128
    b = c // 2
    hg = c % 2
    hrows = slice(hg * HL * 128, (hg + 1) * HL * 128)
    dmrows = slice(hg * DH2, (hg + 1) * DH2)

    d = {}
    d["XT"] = np.ascontiguousarray(x[b].T)
    for nm, W in (("WQ", Wq), ("WK", Wk), ("WV", Wv)):
        d[nm] = np.ascontiguousarray(W[hrows].reshape(HL, 128, D).transpose(0, 2, 1))
    d["BQ"] = np.ascontiguousarray(bq[hrows].reshape(1, -1))
    d["BK"] = np.ascontiguousarray(bk[hrows].reshape(1, -1))
    d["BV"] = np.ascontiguousarray(bv[hrows].reshape(1, -1))
    d["SLP"] = np.ascontiguousarray(slopes[hg * HL:(hg + 1) * HL].reshape(1, HL))
    d["WO"] = np.ascontiguousarray(Wo[dmrows].reshape(MH, 128, D).transpose(0, 2, 1))
    d["BO"] = np.ascontiguousarray(bo[dmrows].reshape(1, -1))
    d["WFN"] = np.ascontiguousarray(Wf[dmrows])
    d["BF2"] = np.ascontiguousarray(bf[dmrows].reshape(1, -1))
    d["XRES"] = np.ascontiguousarray(x[b].T[dmrows, :])
    pidx = np.arange(128)
    kk = np.arange(KD)
    d["CIDX"] = np.ascontiguousarray(
        ((2 * b) * (HL * 128) + kk[None, :] * 128 + pidx[:, None]).astype(np.int32))
    return d


def kernel(L0_attn_hidden_states, L0_attn_Wq, L0_attn_bq, L0_attn_Wk, L0_attn_bk,
           L0_attn_Wv, L0_attn_bv, L0_attn_Wo, L0_attn_bo, L0_attn_alibi_slopes,
           L0_ffn_weight, L0_ffn_bias):
    S, D, HL = S_FULL, D_FULL, H_FULL // 2
    f = lambda a: np.asarray(a, np.float32)
    x = f(L0_attn_hidden_states)
    consts = make_consts(S)
    in_maps = []
    for c in range(NCORES):
        d = make_core_inputs(c, x, f(L0_attn_Wq), f(L0_attn_bq), f(L0_attn_Wk),
                             f(L0_attn_bk), f(L0_attn_Wv), f(L0_attn_bv),
                             f(L0_attn_Wo), f(L0_attn_bo), f(L0_attn_alibi_slopes),
                             f(L0_ffn_weight), f(L0_ffn_bias), S, D, HL)
        d.update(consts)
        in_maps.append(d)

    nc = build_nc(S, D, HL, NCORES)
    res = run_bass_kernel_spmd(nc, in_maps, core_ids=list(range(NCORES)))

    out = np.empty((B_FULL, S, D), np.float32)
    for b in range(B_FULL):
        top = np.asarray(res.results[2 * b]["OUT"], np.float32)
        bot = np.asarray(res.results[2 * b + 1]["OUT"], np.float32)
        out[b] = np.concatenate([top, bot], axis=0).T
    return out


# version 05
# speedup vs baseline: 1.2322x; 1.2322x over previous
import sys
sys.path.insert(0, '/opt/trn_rl_repo')
import math
from contextlib import ExitStack

import numpy as np

import concourse.bass as bass
import concourse.mybir as mybir
from concourse.bass_core_rust import IndirectOffsetOnAxis
from concourse.bass_utils import run_bass_kernel_spmd

F32 = mybir.dt.float32
BF16 = mybir.dt.bfloat16
I32 = mybir.dt.int32
AF = mybir.ActivationFunctionType
ALU = mybir.AluOpType

B_FULL, S_FULL, D_FULL, H_FULL = 4, 4096, 2048, 16
HD = 128
NCORES = 8
EPS = 1e-5
CHKT = 8


def _chunks_for(NT):
    out = []
    for i in range(NT):
        for j0t in range(0, i + 1, CHKT):
            nj = min(CHKT, i + 1 - j0t)
            out.append((i, j0t, nj, j0t + nj - 1 == i))
    return out


class Prog:

    def __init__(self):
        self.q = {k: [] for k in ("sync", "gpsimd", "tensor", "vector", "scalar")}
        self.cnt = {}
        self.queue_sems = {"sync": set(), "gpsimd": set()}

    def inc(self, sem, k):
        self.cnt[sem] = self.cnt.get(sem, 0) + k
        return self.cnt[sem]

    def val(self, sem):
        return self.cnt.get(sem, 0)

    def emit(self, eng, fn):
        self.q[eng].append(fn)


def build_nc(S, D, HL, n_ag_cores, debug=False, phases=6, sub3=99):
    NT = S // 128
    KD = D // 128
    TC = S // 512
    DH2 = D // 2
    MH = DH2 // 128
    SCT = 1.0 / (30.0 * math.sqrt(HD))
    CH = _chunks_for(NT)
    NCH = len(CH)

    nc = bass.Bass("TRN2", target_bir_lowering=False, debug=debug)

    XT = nc.dram_tensor("XT", [D, S], F32, kind="ExternalInput").ap()
    WQ = nc.dram_tensor("WQ", [HL, D, 128], F32, kind="ExternalInput").ap()
    WK = nc.dram_tensor("WK", [HL, D, 128], F32, kind="ExternalInput").ap()
    WV = nc.dram_tensor("WV", [HL, D, 128], F32, kind="ExternalInput").ap()
    BQ = nc.dram_tensor("BQ", [1, HL * 128], F32, kind="ExternalInput").ap()
    BK = nc.dram_tensor("BK", [1, HL * 128], F32, kind="ExternalInput").ap()
    BV = nc.dram_tensor("BV", [1, HL * 128], F32, kind="ExternalInput").ap()
    SLP = nc.dram_tensor("SLP", [1, HL], F32, kind="ExternalInput").ap()
    WO = nc.dram_tensor("WO", [MH, D, 128], F32, kind="ExternalInput").ap()
    BO = nc.dram_tensor("BO", [1, DH2], F32, kind="ExternalInput").ap()
    WFN = nc.dram_tensor("WFN", [DH2, D], F32, kind="ExternalInput").ap()
    BF2 = nc.dram_tensor("BF2", [1, DH2], F32, kind="ExternalInput").ap()
    XRES = nc.dram_tensor("XRES", [DH2, S], F32, kind="ExternalInput").ap()
    CIDX = nc.dram_tensor("CIDX", [128, KD], I32, kind="ExternalInput").ap()
    JPC = nc.dram_tensor("JPC", [128, CHKT * 128], F32, kind="ExternalInput").ap()
    IVEC = nc.dram_tensor("IVEC", [1, NT], F32, kind="ExternalInput").ap()
    TRIU = nc.dram_tensor("TRIU", [128, 128], F32, kind="ExternalInput").ap()
    MSKD = nc.dram_tensor("MSKD", [128, 128], F32, kind="ExternalInput").ap()
    IDN = nc.dram_tensor("IDN", [128, 128], F32, kind="ExternalInput").ap()
    ONESR = nc.dram_tensor("ONESR", [1, 128], F32, kind="ExternalInput").ap()
    ONESQ = nc.dram_tensor("ONESQ", [128, 128], F32, kind="ExternalInput").ap()
    ONES2 = nc.dram_tensor("ONES2", [2, 128], F32, kind="ExternalInput").ap()
    ONESC = nc.dram_tensor("ONESC", [128, 1], F32, kind="ExternalInput").ap()
    ONE1 = nc.dram_tensor("ONE1", [1, 1], F32, kind="ExternalInput").ap()

    OUT = nc.dram_tensor("OUT", [DH2, S], F32, kind="ExternalOutput").ap()

    xnO = nc.dram_tensor("xnO", [D, S], BF16)
    ag_in = nc.dram_tensor("ag_in", [HL * 128, S], BF16)
    if n_ag_cores > 4:
        ag_out = nc.dram_tensor("ag_out", [n_ag_cores * HL * 128, S], BF16,
                                addr_space="Shared")
    else:
        ag_out = nc.dram_tensor("ag_out", [n_ag_cores * HL * 128, S], BF16)
    ctxF = nc.dram_tensor("ctxF", [D, S], BF16)

    P = Prog()

    with ExitStack() as stack:
        def sb(n, s, d):
            return stack.enter_context(nc.sbuf_tensor(n, s, d))

        def psum(n, s, d):
            return stack.enter_context(nc.psum_tensor(n, s, d))

        SEM_NAMES = ["qS", "qG", "qO", "cc",
                     "tLN", "tPRJ", "tTRV", "tMM1", "tMM2", "tTRC", "tTHR", "tOUT",
                     "vLN", "vEV", "vBias", "vCtx", "vCtxT", "vOut", "vTERN", "vMisc",
                     "vSelf", "sSelf",
                     "sTanh", "sExp", "sAbs", "bar",
                     "qGC", "qSTK", "qLa", "qLb", "qMa", "qMb",
                     "qHa", "qHb", "qAa", "qAb", "qWa", "qWb", "qWO",
                     "q3a", "q3b", "qGXa", "qGXb", "qSGa", "qSGb", "qOa", "qOb",
                     "qXa", "qXb"]
        sems = {n: stack.enter_context(nc.semaphore(n)) for n in SEM_NAMES}

        jp = sb("jp", [128, CHKT * 128], F32)
        t1 = sb("t1", [128, CHKT * 128], F32)
        c_all = sb("c_all", [128, HL * NT], F32)
        slope_bc = sb("slope_bc", [128, HL], F32)
        bqT = sb("bqT", [128, HL], F32)
        bkT = sb("bkT", [128, HL], F32)
        bvT = sb("bvT", [128, HL], F32)
        bias_sum = sb("bias_sum", [128, MH], F32)
        triu_bf = sb("triu_bf", [128, 128], BF16)
        mskneg = sb("mskneg", [128, 128], F32)
        tdg = sb("tdg", [128, 128], F32)
        idn_bf = sb("idn_bf", [128, 128], BF16)
        onesr_f = sb("onesr_f", [1, 128], F32)
        onesc_f = sb("onesc_f", [128, 1], F32)
        onesc_bf = sb("onesc_bf", [128, 1], BF16)
        one1_f = sb("one1_f", [1, 1], F32)
        one1_bf = sb("one1_bf", [1, 1], BF16)
        onesr_bf = sb("onesr_bf", [1, 128], BF16)
        onesq_bf = sb("onesq_bf", [128, 128], BF16)
        ones2_bf = sb("ones2_bf", [2, 128], BF16)
        cidx_sb = sb("cidx_sb", [128, KD], I32)
        rvbuf = sb("rvbuf", [128, 1], F32)
        lnrows_bf = sb("lnrows_bf", [1, 2048], BF16)

        ARENA_BYTES = 180 * 1024
        arena = sb("arena", [128, ARENA_BYTES], mybir.dt.uint8)

        def view(off, dt, cols):
            nb = cols * (4 if dt in (F32, I32) else 2)
            assert off % 4 == 0 and off + nb <= ARENA_BYTES, (off, nb)
            return arena[:, off:off + nb].bitcast(dt)

        def mkalloc():
            state = {"o": 0}
            def nxt(dt, cols):
                v = view(state["o"], dt, cols)
                state["o"] += cols * (4 if dt in (F32, I32) else 2)
                state["o"] = (state["o"] + 3) & ~3
                return v
            return nxt, state

        a, st_ = mkalloc()
        slp_row = a(F32, HL)[0:1, :]
        ivec_row = a(F32, NT)[0:1, :]
        bq_row = a(F32, HL * 128)[0:1, :]
        bk_row = a(F32, HL * 128)[0:1, :]
        bv_row = a(F32, HL * 128)[0:1, :]
        bo_row = a(F32, DH2)[0:1, :]
        bf_row = a(F32, DH2)[0:1, :]
        slp_bf = a(BF16, 2 * HL)[0:1, :]
        ivec_bf = a(BF16, NT)[0:1, :]
        brow_bf = a(BF16, 2 * 3 * HL * 128)[0:1, :]
        bsum_row = a(F32, DH2)[0:1, :]
        bsum_bf = a(BF16, 2 * DH2)[0:1, :]
        tmp_row = a(F32, max(DH2, HL * 128))[0:1, :]
        slp2 = a(BF16, 2 * HL)[0:2, 0:HL]
        brow2 = a(BF16, 2 * 3 * HL * 128)[0:2, 0:3 * HL * 128]
        bsum2 = a(BF16, 2 * DH2)[0:2, 0:DH2]

        a, st_ = mkalloc()
        ln_xc = [a(F32, KD * 512), a(F32, KD * 512)]
        ln_xcb = a(BF16, KD * 512)
        ln_xsq = a(BF16, KD * 512)
        ln_xw = a(F32, KD * 512)
        ln_xn = a(BF16, KD * 512)
        mbc = a(F32, 512)
        vbc = a(F32, 512)
        tbc = a(F32, 512)
        assert st_["o"] <= ARENA_BYTES

        a, st_ = mkalloc()
        qbuf = [a(BF16, NT * 128), a(BF16, NT * 128)]
        kbuf = [a(BF16, NT * 128), a(BF16, NT * 128)]
        vbuf = [a(BF16, NT * 129), a(BF16, NT * 129)]
        wqb = [a(BF16, KD * 128), a(BF16, KD * 128)]
        wkb = [a(BF16, KD * 128), a(BF16, KD * 128)]
        wvb = [a(BF16, KD * 128), a(BF16, KD * 128)]
        xslab = [a(BF16, KD * 512), a(BF16, KD * 512)]
        vt_tmp = a(BF16, 512)
        sc_sb = [a(F32, CHKT * 128), a(F32, CHKT * 128)]
        tanh_b = a(F32, CHKT * 128)
        probs = [a(BF16, CHKT * 128), a(BF16, CHKT * 128), a(BF16, CHKT * 128)]
        ctx_nat = a(BF16, NT * 128)
        ctxT_b = a(BF16, NT * 128)
        assert st_["o"] <= ARENA_BYTES, st_["o"]

        a, st_ = mkalloc()
        qt_sb = a(BF16, KD * DH2)
        wo_sb = a(BF16, MH * KD * 128)
        P3_SLAB0 = st_["o"]
        p3_xn = [a(BF16, KD * 512), a(BF16, KD * 512)]
        p3_ctx = [a(BF16, KD * 512), a(BF16, KD * 512)]
        P3_XR0 = st_["o"]
        p3_xr = [a(F32, MH * 512), a(F32, MH * 512)]
        out_st = [a(F32, 512), a(F32, 512)]
        assert st_["o"] <= ARENA_BYTES, st_["o"]

        o3 = P3_SLAB0
        def t3(dt, cols):
            nonlocal o3
            v = view(o3, dt, cols)
            o3 += cols * (4 if dt in (F32, I32) else 2)
            o3 = (o3 + 3) & ~3
            return v
        wfn = [t3(F32, D), t3(F32, D)]
        absn = [t3(F32, D), t3(F32, D)]
        qnat = [t3(BF16, D), t3(BF16, D)]
        m1b = t3(F32, D)
        m2b = t3(F32, D)
        thr_col = t3(F32, 2 * MH)
        redcol = t3(F32, 1)
        GB0 = (max(o3, P3_XR0) + 3) & ~3
        gbuf = [view(GB0, BF16, S), view(GB0 + 2 * S, BF16, S)]
        assert GB0 + 4 * S <= ARENA_BYTES, (GB0, ARENA_BYTES)

        ps_sc2 = [psum("ps_sc0", [128, CHKT * 128], F32),
                  psum("ps_sc1", [128, CHKT * 128], F32)]
        ps_sc = ps_sc2[0]
        ps_q = psum("ps_q", [128, 512], F32)
        ps_k = psum("ps_k", [128, 512], F32)
        ps_v = psum("ps_v", [128, 512], F32)
        ps_tr = psum("ps_tr", [128, 1024], BF16)

        V_SEMS = ("vLN", "vEV", "vBias", "vCtx", "vCtxT", "vOut", "vTERN",
                  "vMisc", "vSelf")
        S_SEMS = ("sTanh", "sExp", "sAbs")
        T_SEMS = ("tLN", "tPRJ", "tTRV", "tMM1", "tMM2", "tTRC", "tTHR", "tOUT")

        def barrier():
            qsw = tuple((s, P.val(s))
                        for s in sorted(P.queue_sems["sync"] | {"qS", "qO"}))
            qgw = tuple((s, P.val(s))
                        for s in sorted(P.queue_sems["gpsimd"] | {"qG"}))
            vw = tuple((s, P.val(s)) for s in V_SEMS)
            sw = tuple((s, P.val(s)) for s in S_SEMS)
            tw = tuple((s, P.val(s)) for s in T_SEMS)
            tgt = P.val("bar") + 5
            P.inc("bar", 5)
            def arm(ws):
                def f(e, ws=ws, t=tgt):
                    for (s, v) in ws:
                        e.wait_ge(sems[s], v)
                    e.nop().then_inc(sems["bar"], 1)
                    e.wait_ge(sems["bar"], t)
                return f
            P.emit("sync", arm(qsw))
            P.emit("gpsimd", arm(qgw))
            P.emit("tensor", arm(tw))
            P.emit("vector", arm(vw))
            P.emit("scalar", arm(sw))

        def dma(eng_q, sem_name, out_ap, in_ap, waits=()):
            P.queue_sems[eng_q].add(sem_name)
            tgt = P.inc(sem_name, 16)
            def fn(e, o=out_ap, i=in_ap, ws=tuple(waits)):
                for (s, v) in ws:
                    e.wait_ge(sems[s], v)
                e.dma_start(out=o, in_=i).then_inc(sems[sem_name], 16)
            P.emit(eng_q, fn)
            return tgt

        for (dst, src) in [(jp[:], JPC), (ivec_row[:], IVEC), (slp_row[:], SLP),
                           (bq_row[:], BQ), (bk_row[:], BK), (bv_row[:], BV),
                           (bo_row[:], BO), (bf_row[:], BF2), (cidx_sb[:], CIDX),
                           (onesr_f[:], ONESR), (onesc_f[:], ONESC), (one1_f[:], ONE1),
                           (mskneg[:], MSKD)]:
            dma("sync", "qS", dst, src)
        for (dst, src) in [(triu_bf[:], TRIU), (idn_bf[:], IDN), (onesc_bf[:], ONESC),
                           (onesr_bf[:], ONESR), (one1_bf[:], ONE1),
                           (onesq_bf[:], ONESQ), (ones2_bf[:], ONES2),
                           (ivec_bf, IVEC)]:
            dma("gpsimd", "qGC", dst, src)
        qs_consts = P.val("qS")
        qg_consts = P.val("qGC")

        vs0 = P.val("vSelf")
        P.inc("vSelf", 19)
        def consts_split(v, qs=qs_consts, qg=qg_consts, vs=vs0):
            I = sems["vSelf"]
            v.wait_ge(sems["qS"], qs)
            v.wait_ge(sems["qGC"], qg)
            v.tensor_copy(slp_bf[0:1, 0:HL], slp_row[:]).then_inc(I, 1)
            v.tensor_copy(brow_bf[0:1, 0:HL * 128], bq_row[:]).then_inc(I, 1)
            v.tensor_copy(brow_bf[0:1, HL * 128:2 * HL * 128], bk_row[:]).then_inc(I, 1)
            v.tensor_copy(brow_bf[0:1, 2 * HL * 128:3 * HL * 128],
                          bv_row[:]).then_inc(I, 1)
            v.tensor_tensor(bsum_row, bo_row[:], bf_row[:], ALU.add).then_inc(I, 1)
            v.wait_ge(I, vs + 5)
            def sub_hi(dst_row, bf_slice, n):
                v.tensor_copy(tmp_row[0:1, 0:n], bf_slice).then_inc(I, 1)
                v.wait_ge(I, P2_c[0] + 1)
                P2_c[0] += 2
                v.tensor_tensor(dst_row, dst_row, tmp_row[0:1, 0:n],
                                ALU.subtract).then_inc(I, 1)
            P2_c = [vs + 5]
            sub_hi(slp_row[:], slp_bf[0:1, 0:HL], HL)
            sub_hi(bq_row[:], brow_bf[0:1, 0:HL * 128], HL * 128)
            sub_hi(bk_row[:], brow_bf[0:1, HL * 128:2 * HL * 128], HL * 128)
            sub_hi(bv_row[:], brow_bf[0:1, 2 * HL * 128:3 * HL * 128], HL * 128)
            v.tensor_copy(bsum_bf[0:1, 0:DH2], bsum_row).then_inc(I, 1)
            v.wait_ge(I, vs + 14)
            v.tensor_copy(slp_bf[0:1, HL:2 * HL], slp_row[:]).then_inc(I, 1)
            v.tensor_copy(brow_bf[0:1, 3 * HL * 128:4 * HL * 128], bq_row[:]).then_inc(I, 1)
            v.tensor_copy(brow_bf[0:1, 4 * HL * 128:5 * HL * 128], bk_row[:]).then_inc(I, 1)
            v.tensor_copy(brow_bf[0:1, 5 * HL * 128:6 * HL * 128], bv_row[:]).then_inc(I, 1)
            v.tensor_copy(tmp_row[0:1, 0:DH2], bsum_bf[0:1, 0:DH2]).then_inc(I, 1)
            v.wait_ge(I, vs + 19)
            v.tensor_tensor(bsum_row, bsum_row, tmp_row[0:1, 0:DH2],
                            ALU.subtract).then_inc(sems["vMisc"], 1)
        P.inc("vMisc", 1)
        vm_sp = P.val("vMisc")
        P.inc("vMisc", 1)
        def consts_split2(v, vm=vm_sp):
            v.wait_ge(sems["vMisc"], vm)
            v.tensor_copy(bsum_bf[0:1, DH2:2 * DH2], bsum_row).then_inc(sems["vMisc"], 1)
        def consts_split_both(v):
            consts_split(v)
            consts_split2(v)
        P.emit("vector", consts_split_both)

        v_split = P.val("vMisc")
        dma("sync", "qSTK", slp2, slp_bf[0:1, 0:2 * HL], waits=[("vMisc", v_split)])
        dma("sync", "qSTK", brow2, brow_bf[0:1, 0:6 * HL * 128], waits=[])
        dma("sync", "qSTK", bsum2, bsum_bf[0:1, 0:2 * DH2], waits=[])
        qs_stack = P.val("qSTK")

        def consts_build(t, qs=qs_stack):
            t.wait_ge(sems["qSTK"], qs)
            t.matmul(ps_sc[:, 0:HL], ones2_bf[:], slp2, start=True, stop=True)
            t.matmul(ps_sc[:, 512:512 + NT], onesr_bf[:], ivec_bf,
                     start=True, stop=True)
            for h in range(HL):
                for (col, off) in ((h, 0), (HL + h, HL * 128), (2 * HL + h, 2 * HL * 128)):
                    t.matmul(ps_q[:, col:col + 1],
                             brow2[0:2, off + h * 128:off + (h + 1) * 128]
                             .transpose() if False else
                             brow2[0:2, off + h * 128:off + (h + 1) * 128],
                             ones2_bf[0:2, 0:1], start=True, stop=True)
            for m in range(MH):
                inst = t.matmul(ps_k[:, m:m + 1], bsum2[0:2, m * 128:(m + 1) * 128],
                                ones2_bf[0:2, 0:1], start=True, stop=True)
                if m == MH - 1:
                    inst.then_inc(sems["tLN"], 1)
        P.inc("tLN", 1)
        P.emit("tensor", consts_build)

        vs1 = P.val("vSelf")
        P.inc("vSelf", 6 + HL)
        def consts_evac(v, vs=vs1):
            I = sems["vSelf"]
            v.wait_ge(sems["tLN"], 1)
            v.tensor_copy(slope_bc[:], ps_sc[:, 0:HL]).then_inc(I, 1)
            v.tensor_copy(bqT[:], ps_q[:, 0:HL]).then_inc(I, 1)
            v.tensor_copy(bkT[:], ps_q[:, HL:2 * HL]).then_inc(I, 1)
            v.tensor_copy(bvT[:], ps_q[:, 2 * HL:3 * HL]).then_inc(I, 1)
            v.tensor_copy(bias_sum[:], ps_k[:, 0:MH]).then_inc(I, 1)
            v.tensor_copy(t1[:, 0:NT], ps_sc[:, 512:512 + NT]).then_inc(I, 1)
            v.wait_ge(I, vs + 6)
            for h in range(HL):
                v.tensor_scalar(c_all[:, h * NT:(h + 1) * NT], t1[:, 0:NT],
                                slope_bc[:, h:h + 1], None, ALU.mult).then_inc(I, 1)
            v.nop().then_inc(sems["vMisc"], 1)
        P.inc("vMisc", 1)
        P.emit("vector", consts_evac)

        barrier()

        xt_t = XT.rearrange("(k p) t -> p k t", p=128)
        xno_t = xnO.ap().rearrange("(k p) t -> p k t", p=128)
        VLN_PER_CHUNK = 4
        for c in range(TC if phases >= 2 else 0):
            w = [("vLN", VLN_PER_CHUNK * (c - 1))] if c >= 2 else []
            qL = "qLa" if c % 2 == 0 else "qLb"
            dma("sync", qL, ln_xc[c % 2][:].rearrange("p (k t) -> p k t", t=512),
                xt_t[:, :, c * 512:(c + 1) * 512], waits=w)
            qs_x = (qL, 16 * (c // 2 + 1))

            def ln_pre(v, c=c, qs=qs_x):
                v.wait_ge(sems[qs[0]], qs[1])
                v.tensor_copy(ln_xcb[:], ln_xc[c % 2][:]).then_inc(sems["vLN"], 1)
            P.inc("vLN", 1)
            P.emit("vector", ln_pre)
            vln_cast = P.val("vLN")

            def ln_sq(s, c=c, qs=qs_x):
                s.wait_ge(sems[qs[0]], qs[1])
                s.activation(ln_xsq[:], ln_xc[c % 2][:], AF.Square).then_inc(sems["sTanh"], 1)
            P.inc("sTanh", 1)
            P.emit("scalar", ln_sq)
            s_sq = P.val("sTanh")

            def ln_mms(t, vc=vln_cast, sq=s_sq):
                t.wait_ge(sems["vLN"], vc)
                t.wait_ge(sems["sTanh"], sq)
                for k in range(KD):
                    t.matmul(ps_q[:, 0:512], onesq_bf[:],
                             ln_xcb[:, k * 512:(k + 1) * 512],
                             start=(k == 0), stop=(k == KD - 1))
                for k in range(KD):
                    inst = t.matmul(ps_k[:, 0:512], onesq_bf[:],
                                    ln_xsq[:, k * 512:(k + 1) * 512],
                                    start=(k == 0), stop=(k == KD - 1))
                    if k == KD - 1:
                        inst.then_inc(sems["tLN"], 1)
            P.inc("tLN", 1)
            P.emit("tensor", ln_mms)
            t_stats = P.val("tLN")

            vsr = P.inc("vSelf", 4)
            def ln_stats1(v, ts=t_stats, vs=vsr):
                v.wait_ge(sems["tLN"], ts)
                v.tensor_single_scalar(mbc[:], ps_q[:, 0:512], 1.0 / D,
                                       ALU.mult).then_inc(sems["vSelf"], 1)
                v.tensor_single_scalar(vbc[:], ps_k[:, 0:512], 1.0 / D,
                                       ALU.mult).then_inc(sems["vSelf"], 1)
                v.wait_ge(sems["vSelf"], vs - 2)
                v.tensor_tensor(tbc[:], mbc[:], mbc[:], ALU.mult).then_inc(
                    sems["vSelf"], 1)
                v.wait_ge(sems["vSelf"], vs - 1)
                v.tensor_tensor(vbc[:], vbc[:], tbc[:], ALU.subtract).then_inc(
                    sems["vSelf"], 1)
                v.wait_ge(sems["vSelf"], vs)
                v.nop().then_inc(sems["vLN"], 1)
            P.inc("vLN", 1)
            P.emit("vector", ln_stats1)
            v_var = P.val("vLN")

            def ln_sqrt(s, vv=v_var):
                s.wait_ge(sems["vLN"], vv)
                s.activation(tbc[:], vbc[:], AF.Sqrt,
                             bias=np.float32(EPS)).then_inc(sems["sTanh"], 1)
            P.inc("sTanh", 1)
            P.emit("scalar", ln_sqrt)
            s_sqrt = P.val("sTanh")

            vsr2 = P.inc("vSelf", 2)
            def ln_stats2(v, ss=s_sqrt, vs=vsr2):
                v.wait_ge(sems["sTanh"], ss)
                v.reciprocal(vbc[:], tbc[:]).then_inc(sems["vSelf"], 1)
                v.wait_ge(sems["vSelf"], vs - 1)
                v.tensor_tensor(tbc[:], mbc[:], vbc[:], ALU.mult).then_inc(
                    sems["vSelf"], 1)
                v.wait_ge(sems["vSelf"], vs)
                v.nop().then_inc(sems["vLN"], 1)
            P.inc("vLN", 1)
            P.emit("vector", ln_stats2)
            v_rs = P.val("vLN")

            qs_prev_spill = (("qMa" if (c - 1) % 2 == 0 else "qMb"),
                             16 * ((c - 1) // 2 + 1)) if c >= 1 else None

            vs_ap = P.inc("vSelf", 1)
            def ln_apply(v, c=c, vr=v_rs, qs=qs_prev_spill, vs=vs_ap):
                v.wait_ge(sems["vLN"], vr)
                if qs is not None:
                    v.wait_ge(sems[qs[0]], qs[1])
                rs_b = vbc[:].unsqueeze(1).broadcast_to([128, KD, 512])
                ms_b = tbc[:].unsqueeze(1).broadcast_to([128, KD, 512])
                xc3 = ln_xc[c % 2][:].rearrange("p (k t) -> p k t", t=512)
                xw3 = ln_xw[:].rearrange("p (k t) -> p k t", t=512)
                xn3 = ln_xn[:].rearrange("p (k t) -> p k t", t=512)
                v.tensor_tensor(xw3, xc3, rs_b, ALU.mult).then_inc(sems["vSelf"], 1)
                v.wait_ge(sems["vSelf"], vs)
                v.tensor_tensor(xn3, xw3, ms_b, ALU.subtract).then_inc(sems["vLN"], 1)
            P.inc("vLN", 1)
            P.emit("vector", ln_apply)
            v_app = P.val("vLN")

            dma("sync", "qMa" if c % 2 == 0 else "qMb",
                xno_t[:, :, c * 512:(c + 1) * 512],
                ln_xn[:].rearrange("p (k t) -> p k t", t=512), waits=[("vLN", v_app)])

        barrier()

        if phases < 3:
            P.emit("sync", lambda e, f=P.val("qS"): e.wait_ge(sems["qS"], f))
            return _finish(nc, P, sems)
        sTanh0 = P.val("sTanh")
        wq_all = WQ.rearrange("h (k p) c -> p (h k) c", p=128)
        wk_all = WK.rearrange("h (k p) c -> p (h k) c", p=128)
        wv_all = WV.rearrange("h (k p) c -> p (h k) c", p=128)

        qg_after_w = {}

        def w_dma(h):
            w = [("tPRJ", (h - 1) * TC)] if h >= 2 else []
            qH = "qHa" if h % 2 == 0 else "qHb"
            for (buf, src) in ((wqb[h % 2], wq_all), (wkb[h % 2], wk_all),
                               (wvb[h % 2], wv_all)):
                dma("gpsimd", qH, buf[:].rearrange("p (k c) -> p k c", c=128),
                    src[:, h * KD:(h + 1) * KD, :], waits=w)
            qg_after_w[h] = (qH, 48 * (h // 2 + 1))

        w_dma(0)
        if HL > 1:
            w_dma(1)

        for h in range(HL):
            t1_wait = sTanh0 + h * NCH
            if h == 0:
                P.inc("vSelf", 2)
            ones_done = P.val("vSelf")
            def t1_build(v, h=h, tw=t1_wait):
                if h == 0:
                    v.wait_ge(sems["vMisc"], 2)
                    v.memset(vbuf[0][:].rearrange("p (n c) -> p n c", c=129)[:, :, 128:129],
                             1.0).then_inc(sems["vSelf"], 1)
                    v.memset(vbuf[1][:].rearrange("p (n c) -> p n c", c=129)[:, :, 128:129],
                             1.0).then_inc(sems["vSelf"], 1)
                else:
                    v.wait_ge(sems["sTanh"], tw)
                v.tensor_scalar(t1[:], jp[:], slope_bc[:, h:h + 1], None,
                                ALU.mult).then_inc(sems["vMisc"], 1)
                v.scalar_tensor_tensor(tdg[:], jp[:, 0:128],
                                       slope_bc[:, h:h + 1], mskneg[:],
                                       ALU.mult, ALU.add).then_inc(sems["vMisc"], 1)
            P.inc("vMisc", 2)
            P.emit("vector", t1_build)

            for tc in range(TC):
                idx = h * TC + tc
                w = [("tPRJ", idx - 1)] if idx >= 2 else []
                qX = "qXa" if idx % 2 == 0 else "qXb"
                dma("sync", qX, xslab[tc % 2][:].rearrange("p (k t) -> p k t", t=512),
                    xno_t[:, :, tc * 512:(tc + 1) * 512], waits=w)
                qs_slab = (qX, 16 * (idx // 2 + 1))

                def prj(t, h=h, tc=tc, qs=qs_slab, qg=qg_after_w[h], s3=sub3):
                    t.wait_ge(sems[qs[0]], qs[1])
                    t.wait_ge(sems[qg[0]], qg[1])
                    if tc == 0 and h > 0 and s3 >= 3:
                        t.wait_ge(sems["vCtx"], h * NT)
                    if h * TC + tc >= 1:
                        t.wait_ge(sems["vEV"], h * TC + tc)
                    sl = xslab[tc % 2]
                    for k in range(KD):
                        st = (k == 0)
                        sp = (k == KD - 1)
                        t.matmul(ps_q[:, 0:512], wqb[h % 2][:, k * 128:(k + 1) * 128],
                                 sl[:, k * 512:(k + 1) * 512], start=st, stop=sp)
                        t.matmul(ps_k[:, 0:512], wkb[h % 2][:, k * 128:(k + 1) * 128],
                                 sl[:, k * 512:(k + 1) * 512], start=st, stop=sp)
                        inst = t.matmul(ps_v[:, 0:512],
                                        wvb[h % 2][:, k * 128:(k + 1) * 128],
                                        sl[:, k * 512:(k + 1) * 512], start=st, stop=sp)
                        if sp:
                            inst.then_inc(sems["tPRJ"], 1)
                P.inc("tPRJ", 1)
                P.emit("tensor", prj)
                t_prj = P.val("tPRJ")

                P.inc("vSelf", 2)
                def prj_ev(v, h=h, tc=tc, tp=t_prj, trv=P.val("tTRV")):
                    v.wait_ge(sems["tPRJ"], tp)
                    if h * TC + tc >= 1:
                        v.wait_ge(sems["tTRV"], trv)
                    v.tensor_scalar(qbuf[h % 2][:, tc * 512:(tc + 1) * 512],
                                    ps_q[:, 0:512], bqT[:, h:h + 1], None,
                                    ALU.add).then_inc(sems["vSelf"], 1)
                    v.tensor_scalar(kbuf[h % 2][:, tc * 512:(tc + 1) * 512],
                                    ps_k[:, 0:512], bkT[:, h:h + 1], None,
                                    ALU.add).then_inc(sems["vSelf"], 1)
                    v.tensor_scalar(vt_tmp[:], ps_v[:, 0:512], bvT[:, h:h + 1],
                                    None, ALU.add).then_inc(sems["vEV"], 1)
                P.inc("vEV", 1)
                P.emit("vector", prj_ev)
                v_ev = P.val("vEV")

                def vtr(t, ve=v_ev, vct=P.val("vCtxT")):
                    t.wait_ge(sems["vEV"], ve)
                    t.wait_ge(sems["vCtxT"], vct)
                    for q4 in range(4):
                        inst = t.transpose(ps_tr[:, q4 * 128:(q4 + 1) * 128],
                                           vt_tmp[:, q4 * 128:(q4 + 1) * 128], idn_bf[:])
                    inst.then_inc(sems["tTRV"], 1)
                P.inc("tTRV", 1)
                P.emit("tensor", vtr)
                t_trv = P.val("tTRV")

                def vev2(v, h=h, tc=tc, tt=t_trv):
                    v.wait_ge(sems["tTRV"], tt)
                    dst = vbuf[h % 2][:, tc * 4 * 129:(tc * 4 + 4) * 129] \
                        .rearrange("p (n c) -> p n c", c=129)[:, :, 0:128]
                    src = ps_tr[:, 0:512].rearrange("p (n c) -> p n c", c=128)
                    v.tensor_copy(dst, src).then_inc(sems["vCtxT"], 1)
                P.inc("vCtxT", 1)
                P.emit("vector", vev2)

            if h + 2 < HL:
                w_dma(h + 2)

            def emit_mm2(pd):
                def mm2(t, pd=pd):
                    t.wait_ge(sems["sExp"], pd["s_e"])
                    if pd["j0t"] == 0:
                        t.wait_ge(sems["vCtx"], pd["hh"] * NT + pd["i"])
                    for jl in range(pd["nj"]):
                        first = (pd["j0t"] == 0 and jl == 0)
                        last = (pd["isdiag"] and jl == pd["nj"] - 1)
                        inst = t.matmul(
                            ps_v[:, 0:129],
                            probs[pd["gci"] % 3][:, jl * 128:(jl + 1) * 128],
                            vbuf[pd["hh"] % 2][:, (pd["j0t"] + jl) * 129:
                                               (pd["j0t"] + jl + 1) * 129],
                            start=first, stop=last)
                        if jl == pd["nj"] - 1:
                            inst.then_inc(sems["tMM2"], 1)
                P.inc("tMM2", 1)
                P.emit("tensor", mm2)
                t_m2 = P.val("tMM2")

                if pd["isdiag"]:
                    vself = P.inc("vSelf", 1)
                    def vctx(v, i=pd["i"], tm2=t_m2, vs=vself):
                        v.wait_ge(sems["tMM2"], tm2)
                        v.reciprocal(rvbuf[:], ps_v[:, 128:129]).then_inc(sems["vSelf"], 1)
                        v.wait_ge(sems["vSelf"], vs)
                        v.tensor_scalar(ctx_nat[:, i * 128:(i + 1) * 128],
                                        ps_v[:, 0:128], rvbuf[:], None,
                                        ALU.mult).then_inc(sems["vCtx"], 1)
                    P.inc("vCtx", 1)
                    P.emit("vector", vctx)

            pend = []
            for ci, (i, j0t, nj, isdiag) in enumerate(CH):
                gci = h * NCH + ci

                def mm1(t, h=h, i=i, j0t=j0t, nj=nj, gci=gci, ci=ci,
                        vsn=P.val("vSelf"), vct=P.val("vCtxT")):
                    if ci == 0:
                        t.wait_ge(sems["vEV"], (h + 1) * TC)
                        t.wait_ge(sems["vSelf"], vsn)
                        t.wait_ge(sems["vCtxT"], vct)
                    if gci >= 2:
                        t.wait_ge(sems["vBias"], gci - 1)
                    for jl in range(nj):
                        inst = t.matmul(
                            ps_sc2[gci % 2][:, jl * 128:(jl + 1) * 128],
                            kbuf[h % 2][:, (j0t + jl) * 128:(j0t + jl + 1) * 128],
                            qbuf[h % 2][:, i * 128:(i + 1) * 128],
                            start=True, stop=True)
                        if jl == nj - 1:
                            inst.then_inc(sems["tMM1"], 1)
                P.inc("tMM1", 1)
                P.emit("tensor", mm1)
                t_mm1 = P.val("tMM1")

                def vbias(v, h=h, i=i, j0t=j0t, nj=nj, gci=gci, tm=t_mm1,
                          isdiag=isdiag, vmc=P.val("vMisc")):
                    v.wait_ge(sems["tMM1"], tm)
                    if gci % NCH == 0:
                        v.wait_ge(sems["vMisc"], vmc)
                    if gci >= 2:
                        v.wait_ge(sems["sTanh"], sTanh0 + gci - 1)
                    nr = nj - 1 if isdiag else nj
                    if nr > 0:
                        inst = v.scalar_tensor_tensor(
                            sc_sb[gci % 2][:, 0:nr * 128],
                            ps_sc2[gci % 2][:, 0:nr * 128],
                            c_all[:, h * NT + (i - j0t):h * NT + (i - j0t) + 1],
                            t1[:, 0:nr * 128], ALU.add, ALU.add)
                    if isdiag:
                        inst = v.tensor_tensor(
                            sc_sb[gci % 2][:, (nj - 1) * 128:nj * 128],
                            ps_sc2[gci % 2][:, (nj - 1) * 128:nj * 128],
                            tdg[:], ALU.add)
                    inst.then_inc(sems["vBias"], 1)
                P.inc("vBias", 1)
                P.emit("vector", vbias)
                v_b = P.val("vBias")

                def stanh(s, nj=nj, gci=gci, vb=v_b):
                    s.wait_ge(sems["vBias"], vb)
                    s.activation(tanh_b[:, 0:nj * 128], sc_sb[gci % 2][:, 0:nj * 128],
                                 AF.Tanh, scale=SCT).then_inc(sems["sTanh"], 1)
                P.inc("sTanh", 1)
                st_now = P.val("sTanh")
                P.emit("scalar", stanh)

                def sexp(s, nj=nj, gci=gci, ss=st_now):
                    if gci >= 3:
                        s.wait_ge(sems["tMM2"], gci - 2)
                    s.wait_ge(sems["sTanh"], ss)
                    s.activation(probs[gci % 3][:, 0:nj * 128], tanh_b[:, 0:nj * 128],
                                 AF.Exp, scale=30.0).then_inc(sems["sExp"], 1)
                P.inc("sExp", 1)
                P.emit("scalar", sexp)
                s_e = P.val("sExp")

                if len(pend) == 2:
                    emit_mm2(pend.pop(0))
                pend.append(dict(hh=h, i=i, j0t=j0t, nj=nj, isdiag=isdiag,
                                 gci=gci, s_e=s_e))
            while pend:
                emit_mm2(pend.pop(0))

            if sub3 < 4:
                continue
            qs_prev_ctx_dma = (("qAa" if (h - 1) % 2 == 0 else "qAb"),
                               16 * ((h - 1) // 2 + 1)) if h >= 1 else None
            NG = (NT + 7) // 8
            for g in range(NG):
                n8 = min(8, NT - g * 8)
                def ctr(t, h=h, g=g, n8=n8, vct=P.val("vCtxT")):
                    t.wait_ge(sems["vCtx"], (h + 1) * NT)
                    t.wait_ge(sems["vCtxT"], vct)
                    for q8 in range(n8):
                        ti = g * 8 + q8
                        inst = t.transpose(ps_tr[:, q8 * 128:(q8 + 1) * 128],
                                           ctx_nat[:, ti * 128:(ti + 1) * 128], idn_bf[:])
                    inst.then_inc(sems["tTRC"], 1)
                P.inc("tTRC", 1)
                P.emit("tensor", ctr)
                t_trc = P.val("tTRC")

                def ctev(v, g=g, n8=n8, tt=t_trc, qs=qs_prev_ctx_dma):
                    v.wait_ge(sems["tTRC"], tt)
                    if g == 0 and qs is not None:
                        v.wait_ge(sems[qs[0]], qs[1])
                    v.tensor_copy(ctxT_b[:, g * 1024:g * 1024 + n8 * 128],
                                  ps_tr[:, 0:n8 * 128]).then_inc(sems["vCtxT"], 1)
                P.inc("vCtxT", 1)
                P.emit("vector", ctev)
            dma("sync", "qAa" if h % 2 == 0 else "qAb",
                ag_in.ap()[h * 128:(h + 1) * 128, :], ctxT_b[:],
                waits=[("vCtxT", P.val("vCtxT"))])

        if debug:
            DBG_Q = nc.dram_tensor("DBG_Q", [128, NT * 128], BF16, kind="ExternalOutput").ap()
            DBG_K = nc.dram_tensor("DBG_K", [128, NT * 128], BF16, kind="ExternalOutput").ap()
            DBG_V = nc.dram_tensor("DBG_V", [128, NT * 129], BF16, kind="ExternalOutput").ap()
            DBG_CN = nc.dram_tensor("DBG_CN", [128, NT * 128], BF16, kind="ExternalOutput").ap()
            dma("sync", "qS", DBG_Q, qbuf[0][:], waits=[("vCtxT", P.val("vCtxT"))])
            dma("sync", "qS", DBG_K, kbuf[0][:], waits=[])
            dma("sync", "qS", DBG_V, vbuf[0][:], waits=[])
            dma("sync", "qS", DBG_CN, ctx_nat[:], waits=[])

        qs_all_ctx = P.val("qS")
        barrier()

        if phases < 5:
            P.emit("gpsimd", lambda e, f=P.val("qG"): e.wait_ge(sems["qG"], f))
            return _finish(nc, P, sems)
        wfn_t = WFN.rearrange("(m p) c -> p m c", p=128)
        for m in range(MH):
            w = [("vTERN", P.val("vTERN") - 1)] if m >= 2 else []
            qW = "qWa" if m % 2 == 0 else "qWb"
            dma("sync", qW, wfn[m % 2][:].unsqueeze(1), wfn_t[:, m:m + 1, :], waits=w)
            qs_w = (qW, 16 * (m // 2 + 1))

            def sabs(s, m=m, qs=qs_w, vt=P.val("vTERN")):
                s.wait_ge(sems[qs[0]], qs[1])
                if m >= 2:
                    s.wait_ge(sems["vTERN"], vt)
                s.activation(absn[m % 2][:], wfn[m % 2][:], AF.Abs).then_inc(
                    sems["sAbs"], 1)
            P.inc("sAbs", 1)
            P.emit("scalar", sabs)
            s_a = P.val("sAbs")

            vst = P.inc("vSelf", 7)
            def tern_m(v, m=m, sa=s_a, vs=vst):
                I = sems["vSelf"]
                v.wait_ge(sems["sAbs"], sa)
                v.reduce_sum(redcol[:], absn[m % 2][:],
                             axis=mybir.AxisListType.X).then_inc(I, 1)
                v.wait_ge(I, vs - 6)
                v.tensor_single_scalar(thr_col[:, m:m + 1], redcol[:], 0.7 / D,
                                       ALU.mult).then_inc(I, 1)
                v.wait_ge(I, vs - 5)
                v.tensor_single_scalar(thr_col[:, MH + m:MH + m + 1],
                                       thr_col[:, m:m + 1], -1.0,
                                       ALU.mult).then_inc(I, 1)
                v.wait_ge(I, vs - 4)
                v.tensor_scalar(m1b[:], wfn[m % 2][:], thr_col[:, MH + m:MH + m + 1],
                                None, ALU.add).then_inc(I, 1)
                v.tensor_scalar(m2b[:], wfn[m % 2][:], thr_col[:, m:m + 1],
                                None, ALU.add).then_inc(I, 1)
                v.wait_ge(I, vs - 2)
                v.tensor_single_scalar(m1b[:], m1b[:], 0.0, ALU.is_gt).then_inc(I, 1)
                v.tensor_single_scalar(m2b[:], m2b[:], 0.0, ALU.is_lt).then_inc(I, 1)
                v.wait_ge(I, vs)
                v.tensor_tensor(qnat[m % 2][:], m1b[:], m2b[:],
                                ALU.subtract).then_inc(sems["vTERN"], 1)
            P.inc("vTERN", 1)
            P.emit("vector", tern_m)
            v_q = P.val("vTERN")

            NG8 = (KD + 7) // 8
            for g8 in range(NG8):
                n8 = min(8, KD - g8 * 8)
                def qtr(t, m=m, g8=g8, n8=n8, vq=v_q, vct=P.val("vCtxT")):
                    t.wait_ge(sems["vTERN"], vq)
                    t.wait_ge(sems["vCtxT"], vct)
                    for q8 in range(n8):
                        kk = g8 * 8 + q8
                        inst = t.transpose(ps_tr[:, q8 * 128:(q8 + 1) * 128],
                                           qnat[m % 2][:, kk * 128:(kk + 1) * 128],
                                           idn_bf[:])
                    inst.then_inc(sems["tTHR"], 1)
                P.inc("tTHR", 1)
                P.emit("tensor", qtr)
                t_q = P.val("tTHR")

                def qev(v, m=m, g8=g8, n8=n8, tq=t_q):
                    v.wait_ge(sems["tTHR"], tq)
                    dst = qt_sb[:, 0:KD * DH2].rearrange("p (k c) -> p k c", c=DH2) \
                        [:, g8 * 8:g8 * 8 + n8, m * 128:(m + 1) * 128]
                    src_ = ps_tr[:, 0:n8 * 128].rearrange("p (n c) -> p n c", c=128)
                    v.tensor_copy(dst, src_).then_inc(sems["vCtxT"], 1)
                P.inc("vCtxT", 1)
                P.emit("vector", qev)
        v_tern_done = P.val("vCtxT")

        wo_all = WO.rearrange("m (k p) c -> p (m k) c", p=128)
        for m in range(MH):
            dma("gpsimd", "qWO", wo_sb[:, m * KD * 128:(m + 1) * KD * 128]
                .rearrange("p (k c) -> p k c", c=128),
                wo_all[:, m * KD:(m + 1) * KD, :])
        qg_wo = P.val("qWO")

        if phases < 4:
            P.emit("sync", lambda e, f=P.val("qS"): e.wait_ge(sems["qS"], f))
            return _finish(nc, P, sems)
        def coll(g, qs=qs_all_ctx):
            g.wait_ge(sems["qS"], qs)
            g.collective_compute(
                "AllGather", ALU.bypass,
                replica_groups=[list(range(n_ag_cores))],
                ins=[ag_in.ap().opt()],
                outs=[ag_out.ap().opt()],
            ).then_inc(sems["cc"], 1)
            g.wait_ge(sems["cc"], 1)
        P.inc("cc", 1)
        P.emit("gpsimd", coll)

        ctxF_t = ctxF.ap().rearrange("(k p) t -> p k t", p=128)
        P.queue_sems["gpsimd"] |= {"qGXa", "qGXb", "qSGa", "qSGb"}
        for k in range(KD):
            qGX = "qGXa" if k % 2 == 0 else "qGXb"
            qSG = "qSGa" if k % 2 == 0 else "qSGb"
            ws = [(qSG, 16 * (k // 2))] if k >= 2 else []
            P.inc(qGX, 16)
            def gat(g, k=k, ws=tuple(ws), qGX=qGX):
                for (s, v) in ws:
                    g.wait_ge(sems[s], v)
                g.indirect_dma_start(
                    gbuf[k % 2][:], None,
                    ag_out.ap(), IndirectOffsetOnAxis(cidx_sb[:, k:k + 1], 0),
                ).then_inc(sems[qGX], 16)
            P.emit("gpsimd", gat)
            g_done = (qGX, 16 * (k // 2 + 1))
            P.inc(qSG, 16)
            def stg(g, k=k, gd=g_done, qSG=qSG):
                g.wait_ge(sems[gd[0]], gd[1])
                g.dma_start(out=ctxF_t[:, k:k + 1, :],
                            in_=gbuf[k % 2][:].unsqueeze(1)).then_inc(sems[qSG], 16)
            P.emit("gpsimd", stg)


        barrier()

        if phases < 6:
            P.emit("sync", lambda e, f=P.val("qS"): e.wait_ge(sems["qS"], f))
            P.emit("gpsimd", lambda e, f=P.val("qG"): e.wait_ge(sems["qG"], f))
            return _finish(nc, P, sems)
        xres_t = XRES.rearrange("(m p) t -> p m t", p=128)
        n_out = 0
        for tc in range(TC):
            w = [("tOUT", (tc - 1) * MH)] if tc >= 2 else []
            q3 = "q3a" if tc % 2 == 0 else "q3b"
            dma("sync", q3, p3_xn[tc % 2][:].rearrange("p (k t) -> p k t", t=512),
                xno_t[:, :, tc * 512:(tc + 1) * 512], waits=w)
            dma("sync", q3, p3_ctx[tc % 2][:].rearrange("p (k t) -> p k t", t=512),
                ctxF_t[:, :, tc * 512:(tc + 1) * 512])
            wv_ = [("vOut", (tc - 1) * MH)] if tc >= 2 else []
            dma("sync", q3, p3_xr[tc % 2][:].rearrange("p (m t) -> p m t", t=512),
                xres_t[:, :, tc * 512:(tc + 1) * 512], waits=wv_)
            qs_slabs = (q3, 48 * (tc // 2 + 1))

            for m in range(MH):
                ps_o = ps_q if m % 2 == 0 else ps_k
                n_out += 1

                def p3mm(t, tc=tc, m=m, qs=qs_slabs, ps_o=ps_o, n=n_out):
                    t.wait_ge(sems[qs[0]], qs[1])
                    if n == 1:
                        t.wait_ge(sems["vCtxT"], v_tern_done)
                        t.wait_ge(sems["qWO"], qg_wo)
                    if n >= 3:
                        t.wait_ge(sems["vOut"], n - 2)
                    for k in range(KD):
                        t.matmul(ps_o[:, 0:512],
                                 qt_sb[:, k * DH2 + m * 128:k * DH2 + (m + 1) * 128],
                                 p3_xn[tc % 2][:, k * 512:(k + 1) * 512],
                                 start=(k == 0), stop=False)
                    for k in range(KD):
                        inst = t.matmul(
                            ps_o[:, 0:512],
                            wo_sb[:, (m * KD + k) * 128:(m * KD + k + 1) * 128],
                            p3_ctx[tc % 2][:, k * 512:(k + 1) * 512],
                            start=False, stop=(k == KD - 1))
                        if k == KD - 1:
                            inst.then_inc(sems["tOUT"], 1)
                P.inc("tOUT", 1)
                P.emit("tensor", p3mm)
                t_o = P.val("tOUT")

                qOp = "qOa" if m % 2 == 0 else "qOb"
                st_guard = (qOp, P.val(qOp)) if P.val(qOp) > 0 else None

                def p3ev(v, tc=tc, m=m, to=t_o, ps_o=ps_o, sg=st_guard):
                    v.wait_ge(sems["tOUT"], to)
                    if sg is not None:
                        v.wait_ge(sems[sg[0]], sg[1])
                    v.scalar_tensor_tensor(out_st[m % 2][:], ps_o[:, 0:512],
                                           bias_sum[:, m:m + 1],
                                           p3_xr[tc % 2][:, m * 512:(m + 1) * 512],
                                           ALU.add, ALU.add).then_inc(sems["vOut"], 1)
                P.inc("vOut", 1)
                P.emit("vector", p3ev)
                v_o = P.val("vOut")

                P.queue_sems["sync"].add(qOp)
                P.inc(qOp, 16)
                def p3st(e, tc=tc, m=m, vo=v_o, qOp=qOp):
                    e.wait_ge(sems["vOut"], vo)
                    e.dma_start(out=OUT[m * 128:(m + 1) * 128, tc * 512:(tc + 1) * 512],
                                in_=out_st[m % 2][:]).then_inc(sems[qOp], 16)
                P.emit("sync", p3st)

        if debug:
            DBG_XNO = nc.dram_tensor("DBG_XNO", [D, S], BF16, kind="ExternalOutput").ap()
            DBG_AGI = nc.dram_tensor("DBG_AGI", [HL * 128, S], BF16, kind="ExternalOutput").ap()
            DBG_CF = nc.dram_tensor("DBG_CF", [D, S], BF16, kind="ExternalOutput").ap()
            DBG_QT = nc.dram_tensor("DBG_QT", [128, KD * DH2], BF16, kind="ExternalOutput").ap()
            DBG_THRB = nc.dram_tensor("DBG_THRB", [128, 2 * MH], F32, kind="ExternalOutput").ap()
            dma("sync", "qS", DBG_QT, qt_sb[:], waits=[("qO", P.val("qO"))])
            dma("sync", "qS", DBG_THRB, thr_col[:], waits=[])
            DBG_T1 = nc.dram_tensor("DBG_T1", [128, CHKT * 128], F32, kind="ExternalOutput").ap()
            DBG_CALL = nc.dram_tensor("DBG_CALL", [128, HL * NT], F32, kind="ExternalOutput").ap()
            DBG_SLB = nc.dram_tensor("DBG_SLB", [128, HL], F32, kind="ExternalOutput").ap()
            DBG_JP = nc.dram_tensor("DBG_JP", [128, CHKT * 128], F32, kind="ExternalOutput").ap()
            dma("sync", "qS", DBG_T1, t1[:], waits=[("qO", P.val("qO"))])
            dma("sync", "qS", DBG_CALL, c_all[:], waits=[])
            dma("sync", "qS", DBG_SLB, slope_bc[:], waits=[])
            dma("sync", "qS", DBG_JP, jp[:], waits=[])
            dma("sync", "qS", DBG_XNO, xnO.ap(), waits=[])
            dma("sync", "qS", DBG_AGI, ag_in.ap(), waits=[])
            dma("sync", "qS", DBG_CF, ctxF.ap(), waits=[])

        drain_s = tuple((s, P.val(s))
                        for s in sorted(P.queue_sems["sync"] | {"qS", "qO"}))
        out_tot = tuple((s, P.val(s)) for s in ("qOa", "qOb"))

        def fin_s(e, ws=drain_s):
            for (s, v) in ws:
                e.wait_ge(sems[s], v)
        P.emit("sync", fin_s)

        def fin_g(e, ws=out_tot):
            for (s, v) in ws:
                e.wait_ge(sems[s], v)
        P.emit("gpsimd", fin_g)

        return _finish(nc, P, sems)


def _finish(nc, P, sems):
    with nc.Block() as block:
        @block.sync
        def _(eng):
            for fn in P.q["sync"]:
                fn(eng)

        @block.gpsimd
        def _(eng):
            for fn in P.q["gpsimd"]:
                fn(eng)

        @block.tensor
        def _(eng):
            for fn in P.q["tensor"]:
                fn(eng)

        @block.vector
        def _(eng):
            for fn in P.q["vector"]:
                fn(eng)

        @block.scalar
        def _(eng):
            for fn in P.q["scalar"]:
                fn(eng)

    return nc


def make_consts(S):
    NT = S // 128
    g = np.arange(CHKT * 128)
    p = np.arange(128)
    JPC = (128 * (g[None, :] // 128) + p[:, None] - (g[None, :] % 128)).astype(np.float32)
    IVEC = (-128.0 * np.arange(NT, dtype=np.float32)).reshape(1, NT)
    TRIU = (np.arange(128)[None, :] >= np.arange(128)[:, None]).astype(np.float32)
    MSKD = np.where(np.arange(128)[:, None] > np.arange(128)[None, :],
                    np.float32(-1e6), np.float32(0.0)).astype(np.float32)
    IDN = np.eye(128, dtype=np.float32)
    return dict(JPC=JPC, IVEC=IVEC, TRIU=TRIU, MSKD=MSKD, IDN=IDN,
                ONESR=np.ones((1, 128), np.float32),
                ONESC=np.ones((128, 1), np.float32),
                ONESQ=np.ones((128, 128), np.float32),
                ONES2=np.ones((2, 128), np.float32),
                ONE1=np.ones((1, 1), np.float32))


def make_core_inputs(c, x, Wq, bq, Wk, bk, Wv, bv, Wo, bo, slopes, Wf, bf, S, D, HL):
    DH2 = D // 2
    MH = DH2 // 128
    KD = D // 128
    b = c // 2
    hg = c % 2
    hrows = slice(hg * HL * 128, (hg + 1) * HL * 128)
    dmrows = slice(hg * DH2, (hg + 1) * DH2)

    d = {}
    d["XT"] = np.ascontiguousarray(x[b].T)
    for nm, W in (("WQ", Wq), ("WK", Wk), ("WV", Wv)):
        d[nm] = np.ascontiguousarray(W[hrows].reshape(HL, 128, D).transpose(0, 2, 1))
    d["BQ"] = np.ascontiguousarray(bq[hrows].reshape(1, -1))
    d["BK"] = np.ascontiguousarray(bk[hrows].reshape(1, -1))
    d["BV"] = np.ascontiguousarray(bv[hrows].reshape(1, -1))
    d["SLP"] = np.ascontiguousarray(slopes[hg * HL:(hg + 1) * HL].reshape(1, HL))
    d["WO"] = np.ascontiguousarray(Wo[dmrows].reshape(MH, 128, D).transpose(0, 2, 1))
    d["BO"] = np.ascontiguousarray(bo[dmrows].reshape(1, -1))
    d["WFN"] = np.ascontiguousarray(Wf[dmrows])
    d["BF2"] = np.ascontiguousarray(bf[dmrows].reshape(1, -1))
    d["XRES"] = np.ascontiguousarray(x[b].T[dmrows, :])
    pidx = np.arange(128)
    kk = np.arange(KD)
    d["CIDX"] = np.ascontiguousarray(
        ((2 * b) * (HL * 128) + kk[None, :] * 128 + pidx[:, None]).astype(np.int32))
    return d


def kernel(L0_attn_hidden_states, L0_attn_Wq, L0_attn_bq, L0_attn_Wk, L0_attn_bk,
           L0_attn_Wv, L0_attn_bv, L0_attn_Wo, L0_attn_bo, L0_attn_alibi_slopes,
           L0_ffn_weight, L0_ffn_bias):
    S, D, HL = S_FULL, D_FULL, H_FULL // 2
    f = lambda a: np.asarray(a, np.float32)
    x = f(L0_attn_hidden_states)
    consts = make_consts(S)
    in_maps = []
    for c in range(NCORES):
        d = make_core_inputs(c, x, f(L0_attn_Wq), f(L0_attn_bq), f(L0_attn_Wk),
                             f(L0_attn_bk), f(L0_attn_Wv), f(L0_attn_bv),
                             f(L0_attn_Wo), f(L0_attn_bo), f(L0_attn_alibi_slopes),
                             f(L0_ffn_weight), f(L0_ffn_bias), S, D, HL)
        d.update(consts)
        in_maps.append(d)

    nc = build_nc(S, D, HL, NCORES)
    res = run_bass_kernel_spmd(nc, in_maps, core_ids=list(range(NCORES)))

    out = np.empty((B_FULL, S, D), np.float32)
    for b in range(B_FULL):
        top = np.asarray(res.results[2 * b]["OUT"], np.float32)
        bot = np.asarray(res.results[2 * b + 1]["OUT"], np.float32)
        out[b] = np.concatenate([top, bot], axis=0).T
    return out


# version 06
# speedup vs baseline: 1.2943x; 1.0504x over previous
import sys
sys.path.insert(0, '/opt/trn_rl_repo')
import math
from contextlib import ExitStack

import numpy as np

import concourse.bass as bass
import concourse.mybir as mybir
from concourse.bass_core_rust import IndirectOffsetOnAxis
from concourse.bass_utils import run_bass_kernel_spmd

F32 = mybir.dt.float32
BF16 = mybir.dt.bfloat16
I32 = mybir.dt.int32
AF = mybir.ActivationFunctionType
ALU = mybir.AluOpType

B_FULL, S_FULL, D_FULL, H_FULL = 4, 4096, 2048, 16
HD = 128
NCORES = 8
EPS = 1e-5
CHKT = 8


def _chunks_for(NT):
    out = []
    for i in range(NT):
        for j0t in range(0, i + 1, CHKT):
            nj = min(CHKT, i + 1 - j0t)
            out.append((i, j0t, nj, j0t + nj - 1 == i))
    return out


class Prog:

    def __init__(self):
        self.q = {k: [] for k in ("sync", "gpsimd", "tensor", "vector", "scalar")}
        self.cnt = {}
        self.queue_sems = {"sync": set(), "gpsimd": set()}

    def inc(self, sem, k):
        self.cnt[sem] = self.cnt.get(sem, 0) + k
        return self.cnt[sem]

    def val(self, sem):
        return self.cnt.get(sem, 0)

    def emit(self, eng, fn):
        self.q[eng].append(fn)


def build_nc(S, D, HL, n_ag_cores, debug=False, phases=6, sub3=99):
    NT = S // 128
    KD = D // 128
    TC = S // 512
    DH2 = D // 2
    MH = DH2 // 128
    SCT = 1.0 / (30.0 * math.sqrt(HD))
    CH = _chunks_for(NT)
    NCH = len(CH)

    nc = bass.Bass("TRN2", target_bir_lowering=False, debug=debug)

    XT = nc.dram_tensor("XT", [D, S], F32, kind="ExternalInput").ap()
    WQ = nc.dram_tensor("WQ", [HL, D, 128], F32, kind="ExternalInput").ap()
    WK = nc.dram_tensor("WK", [HL, D, 128], F32, kind="ExternalInput").ap()
    WV = nc.dram_tensor("WV", [HL, D, 128], F32, kind="ExternalInput").ap()
    BQ = nc.dram_tensor("BQ", [1, HL * 128], F32, kind="ExternalInput").ap()
    BK = nc.dram_tensor("BK", [1, HL * 128], F32, kind="ExternalInput").ap()
    BV = nc.dram_tensor("BV", [1, HL * 128], F32, kind="ExternalInput").ap()
    SLP = nc.dram_tensor("SLP", [1, HL], F32, kind="ExternalInput").ap()
    WO = nc.dram_tensor("WO", [MH, D, 128], F32, kind="ExternalInput").ap()
    BO = nc.dram_tensor("BO", [1, DH2], F32, kind="ExternalInput").ap()
    WFN = nc.dram_tensor("WFN", [DH2, D], F32, kind="ExternalInput").ap()
    BF2 = nc.dram_tensor("BF2", [1, DH2], F32, kind="ExternalInput").ap()
    XRES = nc.dram_tensor("XRES", [DH2, S], F32, kind="ExternalInput").ap()
    CIDX = nc.dram_tensor("CIDX", [128, KD], I32, kind="ExternalInput").ap()
    JPC = nc.dram_tensor("JPC", [128, CHKT * 128], F32, kind="ExternalInput").ap()
    IVEC = nc.dram_tensor("IVEC", [1, NT], F32, kind="ExternalInput").ap()
    TRIU = nc.dram_tensor("TRIU", [128, 128], F32, kind="ExternalInput").ap()
    MSKD = nc.dram_tensor("MSKD", [128, 128], F32, kind="ExternalInput").ap()
    IDN = nc.dram_tensor("IDN", [128, 128], F32, kind="ExternalInput").ap()
    ONESR = nc.dram_tensor("ONESR", [1, 128], F32, kind="ExternalInput").ap()
    ONESQ = nc.dram_tensor("ONESQ", [128, 128], F32, kind="ExternalInput").ap()
    ONES2 = nc.dram_tensor("ONES2", [2, 128], F32, kind="ExternalInput").ap()
    ONESC = nc.dram_tensor("ONESC", [128, 1], F32, kind="ExternalInput").ap()
    ONE1 = nc.dram_tensor("ONE1", [1, 1], F32, kind="ExternalInput").ap()

    OUT = nc.dram_tensor("OUT", [DH2, S], F32, kind="ExternalOutput").ap()

    xnO = nc.dram_tensor("xnO", [D, S], BF16)
    ag_in = nc.dram_tensor("ag_in", [HL * 128, S], BF16)
    R1 = (HL - 1) * 128
    asp = {"addr_space": "Shared"} if n_ag_cores > 4 else {}
    ag_out1 = nc.dram_tensor("ag_out1", [n_ag_cores * R1, S], BF16, **asp)
    ag_out2 = nc.dram_tensor("ag_out2", [n_ag_cores * 128, S], BF16, **asp)
    ctxF = nc.dram_tensor("ctxF", [D, S], BF16)

    P = Prog()

    with ExitStack() as stack:
        def sb(n, s, d):
            return stack.enter_context(nc.sbuf_tensor(n, s, d))

        def psum(n, s, d):
            return stack.enter_context(nc.psum_tensor(n, s, d))

        SEM_NAMES = ["qS", "qG", "qO", "cc",
                     "tLN", "tPRJ", "tTRV", "tMM1", "tMM2", "tTRC", "tTHR", "tOUT",
                     "vLN", "vEV", "vBias", "vCtx", "vCtxT", "vOut", "vTERN", "vMisc",
                     "vSelf", "sSelf",
                     "sTanh", "sExp", "sAbs", "bar",
                     "qGC", "qSTK", "qLa", "qLb", "qMa", "qMb",
                     "qHa", "qHb", "qAa", "qAb", "qWa", "qWb", "qWO",
                     "q3a", "q3b", "qGXa", "qGXb", "qSGa", "qSGb", "qOa", "qOb",
                     "qXa", "qXb", "cc2"]
        sems = {n: stack.enter_context(nc.semaphore(n)) for n in SEM_NAMES}

        jp = sb("jp", [128, CHKT * 128], F32)
        t1 = sb("t1", [128, CHKT * 128], F32)
        c_all = sb("c_all", [128, HL * NT], F32)
        slope_bc = sb("slope_bc", [128, HL], F32)
        bqT = sb("bqT", [128, HL], F32)
        bkT = sb("bkT", [128, HL], F32)
        bvT = sb("bvT", [128, HL], F32)
        bias_sum = sb("bias_sum", [128, MH], F32)
        triu_bf = sb("triu_bf", [128, 128], BF16)
        mskneg = sb("mskneg", [128, 128], F32)
        tdg = sb("tdg", [128, 128], F32)
        idn_bf = sb("idn_bf", [128, 128], BF16)
        onesr_f = sb("onesr_f", [1, 128], F32)
        onesc_f = sb("onesc_f", [128, 1], F32)
        onesc_bf = sb("onesc_bf", [128, 1], BF16)
        one1_f = sb("one1_f", [1, 1], F32)
        one1_bf = sb("one1_bf", [1, 1], BF16)
        onesr_bf = sb("onesr_bf", [1, 128], BF16)
        onesq_bf = sb("onesq_bf", [128, 128], BF16)
        ones2_bf = sb("ones2_bf", [2, 128], BF16)
        cidx_sb = sb("cidx_sb", [128, KD], I32)
        rvbuf = sb("rvbuf", [128, 1], F32)
        lnrows_bf = sb("lnrows_bf", [1, 2048], BF16)

        ARENA_BYTES = 180 * 1024
        arena = sb("arena", [128, ARENA_BYTES], mybir.dt.uint8)

        def view(off, dt, cols):
            nb = cols * (4 if dt in (F32, I32) else 2)
            assert off % 4 == 0 and off + nb <= ARENA_BYTES, (off, nb)
            return arena[:, off:off + nb].bitcast(dt)

        def mkalloc():
            state = {"o": 0}
            def nxt(dt, cols):
                v = view(state["o"], dt, cols)
                state["o"] += cols * (4 if dt in (F32, I32) else 2)
                state["o"] = (state["o"] + 3) & ~3
                return v
            return nxt, state

        a, st_ = mkalloc()
        slp_row = a(F32, HL)[0:1, :]
        ivec_row = a(F32, NT)[0:1, :]
        bq_row = a(F32, HL * 128)[0:1, :]
        bk_row = a(F32, HL * 128)[0:1, :]
        bv_row = a(F32, HL * 128)[0:1, :]
        bo_row = a(F32, DH2)[0:1, :]
        bf_row = a(F32, DH2)[0:1, :]
        slp_bf = a(BF16, 2 * HL)[0:1, :]
        ivec_bf = a(BF16, NT)[0:1, :]
        brow_bf = a(BF16, 2 * 3 * HL * 128)[0:1, :]
        bsum_row = a(F32, DH2)[0:1, :]
        bsum_bf = a(BF16, 2 * DH2)[0:1, :]
        tmp_row = a(F32, max(DH2, HL * 128))[0:1, :]
        slp2 = a(BF16, 2 * HL)[0:2, 0:HL]
        brow2 = a(BF16, 2 * 3 * HL * 128)[0:2, 0:3 * HL * 128]
        bsum2 = a(BF16, 2 * DH2)[0:2, 0:DH2]

        a, st_ = mkalloc()
        ln_xc = [a(F32, KD * 512), a(F32, KD * 512)]
        ln_xcb = a(BF16, KD * 512)
        ln_xsq = a(BF16, KD * 512)
        ln_xw = a(F32, KD * 512)
        ln_xn = a(BF16, KD * 512)
        mbc = a(F32, 512)
        vbc = a(F32, 512)
        tbc = a(F32, 512)
        assert st_["o"] <= ARENA_BYTES

        a, st_ = mkalloc()
        qbuf = [a(BF16, NT * 128), a(BF16, NT * 128)]
        kbuf = [a(BF16, NT * 128), a(BF16, NT * 128)]
        vbuf = [a(BF16, NT * 129), a(BF16, NT * 129)]
        wqb = [a(BF16, KD * 128), a(BF16, KD * 128)]
        wkb = [a(BF16, KD * 128), a(BF16, KD * 128)]
        wvb = [a(BF16, KD * 128), a(BF16, KD * 128)]
        xslab = [a(BF16, KD * 512), a(BF16, KD * 512)]
        vt_tmp = a(BF16, 512)
        sc_sb = [a(F32, CHKT * 128), a(F32, CHKT * 128)]
        tanh_b = a(F32, CHKT * 128)
        probs = [a(BF16, CHKT * 128), a(BF16, CHKT * 128), a(BF16, CHKT * 128)]
        ctx_nat = a(BF16, NT * 128)
        ctxT_b = a(BF16, NT * 128)
        assert st_["o"] <= ARENA_BYTES, st_["o"]

        a, st_ = mkalloc()
        qt_sb = a(BF16, KD * DH2)
        wo_sb = a(BF16, MH * KD * 128)
        P3_SLAB0 = st_["o"]
        p3_xn = [a(BF16, KD * 512), a(BF16, KD * 512)]
        p3_ctx = [a(BF16, KD * 512), a(BF16, KD * 512)]
        P3_XR0 = st_["o"]
        p3_xr = [a(F32, MH * 512), a(F32, MH * 512)]
        out_st = [a(F32, 512), a(F32, 512)]
        assert st_["o"] <= ARENA_BYTES, st_["o"]

        o3 = P3_SLAB0
        def t3(dt, cols):
            nonlocal o3
            v = view(o3, dt, cols)
            o3 += cols * (4 if dt in (F32, I32) else 2)
            o3 = (o3 + 3) & ~3
            return v
        wfn = [t3(F32, D), t3(F32, D)]
        absn = [t3(F32, D), t3(F32, D)]
        qnat = [t3(BF16, D), t3(BF16, D)]
        m1b = t3(F32, D)
        m2b = t3(F32, D)
        thr_col = t3(F32, 2 * MH)
        redcol = t3(F32, 1)
        GB0 = (max(o3, P3_XR0) + 3) & ~3
        gbuf = [view(GB0, BF16, S), view(GB0 + 2 * S, BF16, S)]
        assert GB0 + 4 * S <= ARENA_BYTES, (GB0, ARENA_BYTES)

        ps_sc2 = [psum("ps_sc0", [128, CHKT * 128], F32),
                  psum("ps_sc1", [128, CHKT * 128], F32)]
        ps_sc = ps_sc2[0]
        ps_q = psum("ps_q", [128, 512], F32)
        ps_k = psum("ps_k", [128, 512], F32)
        ps_v = psum("ps_v", [128, 512], F32)
        ps_tr = psum("ps_tr", [128, 1024], BF16)

        V_SEMS = ("vLN", "vEV", "vBias", "vCtx", "vCtxT", "vOut", "vTERN",
                  "vMisc", "vSelf")
        S_SEMS = ("sTanh", "sExp", "sAbs")
        T_SEMS = ("tLN", "tPRJ", "tTRV", "tMM1", "tMM2", "tTRC", "tTHR", "tOUT")

        def barrier():
            qsw = tuple((s, P.val(s))
                        for s in sorted(P.queue_sems["sync"] | {"qS", "qO"}))
            qgw = tuple((s, P.val(s))
                        for s in sorted(P.queue_sems["gpsimd"] | {"qG"}))
            vw = tuple((s, P.val(s)) for s in V_SEMS)
            sw = tuple((s, P.val(s)) for s in S_SEMS)
            tw = tuple((s, P.val(s)) for s in T_SEMS)
            tgt = P.val("bar") + 5
            P.inc("bar", 5)
            def arm(ws):
                def f(e, ws=ws, t=tgt):
                    for (s, v) in ws:
                        e.wait_ge(sems[s], v)
                    e.nop().then_inc(sems["bar"], 1)
                    e.wait_ge(sems["bar"], t)
                return f
            P.emit("sync", arm(qsw))
            P.emit("gpsimd", arm(qgw))
            P.emit("tensor", arm(tw))
            P.emit("vector", arm(vw))
            P.emit("scalar", arm(sw))

        def dma(eng_q, sem_name, out_ap, in_ap, waits=()):
            P.queue_sems[eng_q].add(sem_name)
            tgt = P.inc(sem_name, 16)
            def fn(e, o=out_ap, i=in_ap, ws=tuple(waits)):
                for (s, v) in ws:
                    e.wait_ge(sems[s], v)
                e.dma_start(out=o, in_=i).then_inc(sems[sem_name], 16)
            P.emit(eng_q, fn)
            return tgt

        for (dst, src) in [(jp[:], JPC), (ivec_row[:], IVEC), (slp_row[:], SLP),
                           (bq_row[:], BQ), (bk_row[:], BK), (bv_row[:], BV),
                           (bo_row[:], BO), (bf_row[:], BF2), (cidx_sb[:], CIDX),
                           (onesr_f[:], ONESR), (onesc_f[:], ONESC), (one1_f[:], ONE1),
                           (mskneg[:], MSKD)]:
            dma("sync", "qS", dst, src)
        for (dst, src) in [(triu_bf[:], TRIU), (idn_bf[:], IDN), (onesc_bf[:], ONESC),
                           (onesr_bf[:], ONESR), (one1_bf[:], ONE1),
                           (onesq_bf[:], ONESQ), (ones2_bf[:], ONES2),
                           (ivec_bf, IVEC)]:
            dma("gpsimd", "qGC", dst, src)
        qs_consts = P.val("qS")
        qg_consts = P.val("qGC")

        vs0 = P.val("vSelf")
        P.inc("vSelf", 19)
        def consts_split(v, qs=qs_consts, qg=qg_consts, vs=vs0):
            I = sems["vSelf"]
            v.wait_ge(sems["qS"], qs)
            v.wait_ge(sems["qGC"], qg)
            v.tensor_copy(slp_bf[0:1, 0:HL], slp_row[:]).then_inc(I, 1)
            v.tensor_copy(brow_bf[0:1, 0:HL * 128], bq_row[:]).then_inc(I, 1)
            v.tensor_copy(brow_bf[0:1, HL * 128:2 * HL * 128], bk_row[:]).then_inc(I, 1)
            v.tensor_copy(brow_bf[0:1, 2 * HL * 128:3 * HL * 128],
                          bv_row[:]).then_inc(I, 1)
            v.tensor_tensor(bsum_row, bo_row[:], bf_row[:], ALU.add).then_inc(I, 1)
            v.wait_ge(I, vs + 5)
            def sub_hi(dst_row, bf_slice, n):
                v.tensor_copy(tmp_row[0:1, 0:n], bf_slice).then_inc(I, 1)
                v.wait_ge(I, P2_c[0] + 1)
                P2_c[0] += 2
                v.tensor_tensor(dst_row, dst_row, tmp_row[0:1, 0:n],
                                ALU.subtract).then_inc(I, 1)
            P2_c = [vs + 5]
            sub_hi(slp_row[:], slp_bf[0:1, 0:HL], HL)
            sub_hi(bq_row[:], brow_bf[0:1, 0:HL * 128], HL * 128)
            sub_hi(bk_row[:], brow_bf[0:1, HL * 128:2 * HL * 128], HL * 128)
            sub_hi(bv_row[:], brow_bf[0:1, 2 * HL * 128:3 * HL * 128], HL * 128)
            v.tensor_copy(bsum_bf[0:1, 0:DH2], bsum_row).then_inc(I, 1)
            v.wait_ge(I, vs + 14)
            v.tensor_copy(slp_bf[0:1, HL:2 * HL], slp_row[:]).then_inc(I, 1)
            v.tensor_copy(brow_bf[0:1, 3 * HL * 128:4 * HL * 128], bq_row[:]).then_inc(I, 1)
            v.tensor_copy(brow_bf[0:1, 4 * HL * 128:5 * HL * 128], bk_row[:]).then_inc(I, 1)
            v.tensor_copy(brow_bf[0:1, 5 * HL * 128:6 * HL * 128], bv_row[:]).then_inc(I, 1)
            v.tensor_copy(tmp_row[0:1, 0:DH2], bsum_bf[0:1, 0:DH2]).then_inc(I, 1)
            v.wait_ge(I, vs + 19)
            v.tensor_tensor(bsum_row, bsum_row, tmp_row[0:1, 0:DH2],
                            ALU.subtract).then_inc(sems["vMisc"], 1)
        P.inc("vMisc", 1)
        vm_sp = P.val("vMisc")
        P.inc("vMisc", 1)
        def consts_split2(v, vm=vm_sp):
            v.wait_ge(sems["vMisc"], vm)
            v.tensor_copy(bsum_bf[0:1, DH2:2 * DH2], bsum_row).then_inc(sems["vMisc"], 1)
        def consts_split_both(v):
            consts_split(v)
            consts_split2(v)
        P.emit("vector", consts_split_both)

        v_split = P.val("vMisc")
        dma("sync", "qSTK", slp2, slp_bf[0:1, 0:2 * HL], waits=[("vMisc", v_split)])
        dma("sync", "qSTK", brow2, brow_bf[0:1, 0:6 * HL * 128], waits=[])
        dma("sync", "qSTK", bsum2, bsum_bf[0:1, 0:2 * DH2], waits=[])
        qs_stack = P.val("qSTK")

        def consts_build(t, qs=qs_stack):
            t.wait_ge(sems["qSTK"], qs)
            t.matmul(ps_sc[:, 0:HL], ones2_bf[:], slp2, start=True, stop=True)
            t.matmul(ps_sc[:, 512:512 + NT], onesr_bf[:], ivec_bf,
                     start=True, stop=True)
            for h in range(HL):
                for (col, off) in ((h, 0), (HL + h, HL * 128), (2 * HL + h, 2 * HL * 128)):
                    t.matmul(ps_q[:, col:col + 1],
                             brow2[0:2, off + h * 128:off + (h + 1) * 128]
                             .transpose() if False else
                             brow2[0:2, off + h * 128:off + (h + 1) * 128],
                             ones2_bf[0:2, 0:1], start=True, stop=True)
            for m in range(MH):
                inst = t.matmul(ps_k[:, m:m + 1], bsum2[0:2, m * 128:(m + 1) * 128],
                                ones2_bf[0:2, 0:1], start=True, stop=True)
                if m == MH - 1:
                    inst.then_inc(sems["tLN"], 1)
        P.inc("tLN", 1)
        P.emit("tensor", consts_build)

        vs1 = P.val("vSelf")
        P.inc("vSelf", 6 + HL)
        def consts_evac(v, vs=vs1):
            I = sems["vSelf"]
            v.wait_ge(sems["tLN"], 1)
            v.tensor_copy(slope_bc[:], ps_sc[:, 0:HL]).then_inc(I, 1)
            v.tensor_copy(bqT[:], ps_q[:, 0:HL]).then_inc(I, 1)
            v.tensor_copy(bkT[:], ps_q[:, HL:2 * HL]).then_inc(I, 1)
            v.tensor_copy(bvT[:], ps_q[:, 2 * HL:3 * HL]).then_inc(I, 1)
            v.tensor_copy(bias_sum[:], ps_k[:, 0:MH]).then_inc(I, 1)
            v.tensor_copy(t1[:, 0:NT], ps_sc[:, 512:512 + NT]).then_inc(I, 1)
            v.wait_ge(I, vs + 6)
            for h in range(HL):
                v.tensor_scalar(c_all[:, h * NT:(h + 1) * NT], t1[:, 0:NT],
                                slope_bc[:, h:h + 1], None, ALU.mult).then_inc(I, 1)
            v.nop().then_inc(sems["vMisc"], 1)
        P.inc("vMisc", 1)
        P.emit("vector", consts_evac)

        barrier()

        xt_t = XT.rearrange("(k p) t -> p k t", p=128)
        xno_t = xnO.ap().rearrange("(k p) t -> p k t", p=128)
        VLN_PER_CHUNK = 4
        for c in range(TC if phases >= 2 else 0):
            w = [("vLN", VLN_PER_CHUNK * (c - 1))] if c >= 2 else []
            qL = "qLa" if c % 2 == 0 else "qLb"
            dma("sync", qL, ln_xc[c % 2][:].rearrange("p (k t) -> p k t", t=512),
                xt_t[:, :, c * 512:(c + 1) * 512], waits=w)
            qs_x = (qL, 16 * (c // 2 + 1))

            def ln_pre(v, c=c, qs=qs_x):
                v.wait_ge(sems[qs[0]], qs[1])
                v.tensor_copy(ln_xcb[:], ln_xc[c % 2][:]).then_inc(sems["vLN"], 1)
            P.inc("vLN", 1)
            P.emit("vector", ln_pre)
            vln_cast = P.val("vLN")

            def ln_sq(s, c=c, qs=qs_x):
                s.wait_ge(sems[qs[0]], qs[1])
                s.activation(ln_xsq[:], ln_xc[c % 2][:], AF.Square).then_inc(sems["sTanh"], 1)
            P.inc("sTanh", 1)
            P.emit("scalar", ln_sq)
            s_sq = P.val("sTanh")

            def ln_mms(t, vc=vln_cast, sq=s_sq):
                t.wait_ge(sems["vLN"], vc)
                t.wait_ge(sems["sTanh"], sq)
                for k in range(KD):
                    t.matmul(ps_q[:, 0:512], onesq_bf[:],
                             ln_xcb[:, k * 512:(k + 1) * 512],
                             start=(k == 0), stop=(k == KD - 1))
                for k in range(KD):
                    inst = t.matmul(ps_k[:, 0:512], onesq_bf[:],
                                    ln_xsq[:, k * 512:(k + 1) * 512],
                                    start=(k == 0), stop=(k == KD - 1))
                    if k == KD - 1:
                        inst.then_inc(sems["tLN"], 1)
            P.inc("tLN", 1)
            P.emit("tensor", ln_mms)
            t_stats = P.val("tLN")

            vsr = P.inc("vSelf", 4)
            def ln_stats1(v, ts=t_stats, vs=vsr):
                v.wait_ge(sems["tLN"], ts)
                v.tensor_single_scalar(mbc[:], ps_q[:, 0:512], 1.0 / D,
                                       ALU.mult).then_inc(sems["vSelf"], 1)
                v.tensor_single_scalar(vbc[:], ps_k[:, 0:512], 1.0 / D,
                                       ALU.mult).then_inc(sems["vSelf"], 1)
                v.wait_ge(sems["vSelf"], vs - 2)
                v.tensor_tensor(tbc[:], mbc[:], mbc[:], ALU.mult).then_inc(
                    sems["vSelf"], 1)
                v.wait_ge(sems["vSelf"], vs - 1)
                v.tensor_tensor(vbc[:], vbc[:], tbc[:], ALU.subtract).then_inc(
                    sems["vSelf"], 1)
                v.wait_ge(sems["vSelf"], vs)
                v.nop().then_inc(sems["vLN"], 1)
            P.inc("vLN", 1)
            P.emit("vector", ln_stats1)
            v_var = P.val("vLN")

            def ln_sqrt(s, vv=v_var):
                s.wait_ge(sems["vLN"], vv)
                s.activation(tbc[:], vbc[:], AF.Sqrt,
                             bias=np.float32(EPS)).then_inc(sems["sTanh"], 1)
            P.inc("sTanh", 1)
            P.emit("scalar", ln_sqrt)
            s_sqrt = P.val("sTanh")

            vsr2 = P.inc("vSelf", 2)
            def ln_stats2(v, ss=s_sqrt, vs=vsr2):
                v.wait_ge(sems["sTanh"], ss)
                v.reciprocal(vbc[:], tbc[:]).then_inc(sems["vSelf"], 1)
                v.wait_ge(sems["vSelf"], vs - 1)
                v.tensor_tensor(tbc[:], mbc[:], vbc[:], ALU.mult).then_inc(
                    sems["vSelf"], 1)
                v.wait_ge(sems["vSelf"], vs)
                v.nop().then_inc(sems["vLN"], 1)
            P.inc("vLN", 1)
            P.emit("vector", ln_stats2)
            v_rs = P.val("vLN")

            qs_prev_spill = (("qMa" if (c - 1) % 2 == 0 else "qMb"),
                             16 * ((c - 1) // 2 + 1)) if c >= 1 else None

            vs_ap = P.inc("vSelf", 1)
            def ln_apply(v, c=c, vr=v_rs, qs=qs_prev_spill, vs=vs_ap):
                v.wait_ge(sems["vLN"], vr)
                if qs is not None:
                    v.wait_ge(sems[qs[0]], qs[1])
                rs_b = vbc[:].unsqueeze(1).broadcast_to([128, KD, 512])
                ms_b = tbc[:].unsqueeze(1).broadcast_to([128, KD, 512])
                xc3 = ln_xc[c % 2][:].rearrange("p (k t) -> p k t", t=512)
                xw3 = ln_xw[:].rearrange("p (k t) -> p k t", t=512)
                xn3 = ln_xn[:].rearrange("p (k t) -> p k t", t=512)
                v.tensor_tensor(xw3, xc3, rs_b, ALU.mult).then_inc(sems["vSelf"], 1)
                v.wait_ge(sems["vSelf"], vs)
                v.tensor_tensor(xn3, xw3, ms_b, ALU.subtract).then_inc(sems["vLN"], 1)
            P.inc("vLN", 1)
            P.emit("vector", ln_apply)
            v_app = P.val("vLN")

            dma("sync", "qMa" if c % 2 == 0 else "qMb",
                xno_t[:, :, c * 512:(c + 1) * 512],
                ln_xn[:].rearrange("p (k t) -> p k t", t=512), waits=[("vLN", v_app)])

        barrier()

        if phases < 3:
            P.emit("sync", lambda e, f=P.val("qS"): e.wait_ge(sems["qS"], f))
            return _finish(nc, P, sems)
        sTanh0 = P.val("sTanh")
        wq_all = WQ.rearrange("h (k p) c -> p (h k) c", p=128)
        wk_all = WK.rearrange("h (k p) c -> p (h k) c", p=128)
        wv_all = WV.rearrange("h (k p) c -> p (h k) c", p=128)

        qg_after_w = {}

        def w_dma(h):
            w = [("tPRJ", (h - 1) * TC)] if h >= 2 else []
            qH = "qHa" if h % 2 == 0 else "qHb"
            for (buf, src) in ((wqb[h % 2], wq_all), (wkb[h % 2], wk_all),
                               (wvb[h % 2], wv_all)):
                dma("gpsimd", qH, buf[:].rearrange("p (k c) -> p k c", c=128),
                    src[:, h * KD:(h + 1) * KD, :], waits=w)
            qg_after_w[h] = (qH, 48 * (h // 2 + 1))

        w_dma(0)
        if HL > 1:
            w_dma(1)

        for h in range(HL):
            t1_wait = sTanh0 + h * NCH
            if h == 0:
                P.inc("vSelf", 2)
            ones_done = P.val("vSelf")
            def t1_build(v, h=h, tw=t1_wait):
                if h == 0:
                    v.wait_ge(sems["vMisc"], 2)
                    v.memset(vbuf[0][:].rearrange("p (n c) -> p n c", c=129)[:, :, 128:129],
                             1.0).then_inc(sems["vSelf"], 1)
                    v.memset(vbuf[1][:].rearrange("p (n c) -> p n c", c=129)[:, :, 128:129],
                             1.0).then_inc(sems["vSelf"], 1)
                else:
                    v.wait_ge(sems["sTanh"], tw)
                v.tensor_scalar(t1[:], jp[:], slope_bc[:, h:h + 1], None,
                                ALU.mult).then_inc(sems["vMisc"], 1)
                v.scalar_tensor_tensor(tdg[:], jp[:, 0:128],
                                       slope_bc[:, h:h + 1], mskneg[:],
                                       ALU.mult, ALU.add).then_inc(sems["vMisc"], 1)
            P.inc("vMisc", 2)
            P.emit("vector", t1_build)

            for tc in range(TC):
                idx = h * TC + tc
                w = [("tPRJ", idx - 1)] if idx >= 2 else []
                qX = "qXa" if idx % 2 == 0 else "qXb"
                dma("sync", qX, xslab[tc % 2][:].rearrange("p (k t) -> p k t", t=512),
                    xno_t[:, :, tc * 512:(tc + 1) * 512], waits=w)
                qs_slab = (qX, 16 * (idx // 2 + 1))

                def prj(t, h=h, tc=tc, qs=qs_slab, qg=qg_after_w[h], s3=sub3):
                    t.wait_ge(sems[qs[0]], qs[1])
                    t.wait_ge(sems[qg[0]], qg[1])
                    if tc == 0 and h > 0 and s3 >= 3:
                        t.wait_ge(sems["vCtx"], h * NT)
                    if h * TC + tc >= 1:
                        t.wait_ge(sems["vEV"], h * TC + tc)
                    sl = xslab[tc % 2]
                    for k in range(KD):
                        st = (k == 0)
                        sp = (k == KD - 1)
                        t.matmul(ps_q[:, 0:512], wqb[h % 2][:, k * 128:(k + 1) * 128],
                                 sl[:, k * 512:(k + 1) * 512], start=st, stop=sp)
                        t.matmul(ps_k[:, 0:512], wkb[h % 2][:, k * 128:(k + 1) * 128],
                                 sl[:, k * 512:(k + 1) * 512], start=st, stop=sp)
                        inst = t.matmul(ps_v[:, 0:512],
                                        wvb[h % 2][:, k * 128:(k + 1) * 128],
                                        sl[:, k * 512:(k + 1) * 512], start=st, stop=sp)
                        if sp:
                            inst.then_inc(sems["tPRJ"], 1)
                P.inc("tPRJ", 1)
                P.emit("tensor", prj)
                t_prj = P.val("tPRJ")

                P.inc("vSelf", 2)
                def prj_ev(v, h=h, tc=tc, tp=t_prj, trv=P.val("tTRV")):
                    v.wait_ge(sems["tPRJ"], tp)
                    if h * TC + tc >= 1:
                        v.wait_ge(sems["tTRV"], trv)
                    v.tensor_scalar(qbuf[h % 2][:, tc * 512:(tc + 1) * 512],
                                    ps_q[:, 0:512], bqT[:, h:h + 1], None,
                                    ALU.add).then_inc(sems["vSelf"], 1)
                    v.tensor_scalar(kbuf[h % 2][:, tc * 512:(tc + 1) * 512],
                                    ps_k[:, 0:512], bkT[:, h:h + 1], None,
                                    ALU.add).then_inc(sems["vSelf"], 1)
                    v.tensor_scalar(vt_tmp[:], ps_v[:, 0:512], bvT[:, h:h + 1],
                                    None, ALU.add).then_inc(sems["vEV"], 1)
                P.inc("vEV", 1)
                P.emit("vector", prj_ev)
                v_ev = P.val("vEV")

                def vtr(t, ve=v_ev, vct=P.val("vCtxT")):
                    t.wait_ge(sems["vEV"], ve)
                    t.wait_ge(sems["vCtxT"], vct)
                    for q4 in range(4):
                        inst = t.transpose(ps_tr[:, q4 * 128:(q4 + 1) * 128],
                                           vt_tmp[:, q4 * 128:(q4 + 1) * 128], idn_bf[:])
                    inst.then_inc(sems["tTRV"], 1)
                P.inc("tTRV", 1)
                P.emit("tensor", vtr)
                t_trv = P.val("tTRV")

                def vev2(v, h=h, tc=tc, tt=t_trv):
                    v.wait_ge(sems["tTRV"], tt)
                    dst = vbuf[h % 2][:, tc * 4 * 129:(tc * 4 + 4) * 129] \
                        .rearrange("p (n c) -> p n c", c=129)[:, :, 0:128]
                    src = ps_tr[:, 0:512].rearrange("p (n c) -> p n c", c=128)
                    v.tensor_copy(dst, src).then_inc(sems["vCtxT"], 1)
                P.inc("vCtxT", 1)
                P.emit("vector", vev2)

            if h + 2 < HL:
                w_dma(h + 2)

            def emit_mm2(pd):
                def mm2(t, pd=pd):
                    t.wait_ge(sems["sExp"], pd["s_e"])
                    if pd["j0t"] == 0:
                        t.wait_ge(sems["vCtx"], pd["hh"] * NT + pd["i"])
                    for jl in range(pd["nj"]):
                        first = (pd["j0t"] == 0 and jl == 0)
                        last = (pd["isdiag"] and jl == pd["nj"] - 1)
                        inst = t.matmul(
                            ps_v[:, 0:129],
                            probs[pd["gci"] % 3][:, jl * 128:(jl + 1) * 128],
                            vbuf[pd["hh"] % 2][:, (pd["j0t"] + jl) * 129:
                                               (pd["j0t"] + jl + 1) * 129],
                            start=first, stop=last)
                        if jl == pd["nj"] - 1:
                            inst.then_inc(sems["tMM2"], 1)
                P.inc("tMM2", 1)
                P.emit("tensor", mm2)
                t_m2 = P.val("tMM2")

                if pd["isdiag"]:
                    vself = P.inc("vSelf", 1)
                    def vctx(v, i=pd["i"], tm2=t_m2, vs=vself):
                        v.wait_ge(sems["tMM2"], tm2)
                        v.reciprocal(rvbuf[:], ps_v[:, 128:129]).then_inc(sems["vSelf"], 1)
                        v.wait_ge(sems["vSelf"], vs)
                        v.tensor_scalar(ctx_nat[:, i * 128:(i + 1) * 128],
                                        ps_v[:, 0:128], rvbuf[:], None,
                                        ALU.mult).then_inc(sems["vCtx"], 1)
                    P.inc("vCtx", 1)
                    P.emit("vector", vctx)

            pend = []
            for ci, (i, j0t, nj, isdiag) in enumerate(CH):
                gci = h * NCH + ci

                def mm1(t, h=h, i=i, j0t=j0t, nj=nj, gci=gci, ci=ci,
                        vsn=P.val("vSelf"), vct=P.val("vCtxT")):
                    if ci == 0:
                        t.wait_ge(sems["vEV"], (h + 1) * TC)
                        t.wait_ge(sems["vSelf"], vsn)
                        t.wait_ge(sems["vCtxT"], vct)
                    if gci >= 2:
                        t.wait_ge(sems["vBias"], gci - 1)
                    for jl in range(nj):
                        inst = t.matmul(
                            ps_sc2[gci % 2][:, jl * 128:(jl + 1) * 128],
                            kbuf[h % 2][:, (j0t + jl) * 128:(j0t + jl + 1) * 128],
                            qbuf[h % 2][:, i * 128:(i + 1) * 128],
                            start=True, stop=True)
                        if jl == nj - 1:
                            inst.then_inc(sems["tMM1"], 1)
                P.inc("tMM1", 1)
                P.emit("tensor", mm1)
                t_mm1 = P.val("tMM1")

                def vbias(v, h=h, i=i, j0t=j0t, nj=nj, gci=gci, tm=t_mm1,
                          isdiag=isdiag, vmc=P.val("vMisc")):
                    v.wait_ge(sems["tMM1"], tm)
                    if gci % NCH == 0:
                        v.wait_ge(sems["vMisc"], vmc)
                    if gci >= 2:
                        v.wait_ge(sems["sTanh"], sTanh0 + gci - 1)
                    nr = nj - 1 if isdiag else nj
                    if nr > 0:
                        inst = v.scalar_tensor_tensor(
                            sc_sb[gci % 2][:, 0:nr * 128],
                            ps_sc2[gci % 2][:, 0:nr * 128],
                            c_all[:, h * NT + (i - j0t):h * NT + (i - j0t) + 1],
                            t1[:, 0:nr * 128], ALU.add, ALU.add)
                    if isdiag:
                        inst = v.tensor_tensor(
                            sc_sb[gci % 2][:, (nj - 1) * 128:nj * 128],
                            ps_sc2[gci % 2][:, (nj - 1) * 128:nj * 128],
                            tdg[:], ALU.add)
                    inst.then_inc(sems["vBias"], 1)
                P.inc("vBias", 1)
                P.emit("vector", vbias)
                v_b = P.val("vBias")

                def stanh(s, nj=nj, gci=gci, vb=v_b):
                    s.wait_ge(sems["vBias"], vb)
                    s.activation(tanh_b[:, 0:nj * 128], sc_sb[gci % 2][:, 0:nj * 128],
                                 AF.Tanh, scale=SCT).then_inc(sems["sTanh"], 1)
                P.inc("sTanh", 1)
                st_now = P.val("sTanh")
                P.emit("scalar", stanh)

                def sexp(s, nj=nj, gci=gci, ss=st_now):
                    if gci >= 3:
                        s.wait_ge(sems["tMM2"], gci - 2)
                    s.wait_ge(sems["sTanh"], ss)
                    s.activation(probs[gci % 3][:, 0:nj * 128], tanh_b[:, 0:nj * 128],
                                 AF.Exp, scale=30.0).then_inc(sems["sExp"], 1)
                P.inc("sExp", 1)
                P.emit("scalar", sexp)
                s_e = P.val("sExp")

                if len(pend) == 2:
                    emit_mm2(pend.pop(0))
                pend.append(dict(hh=h, i=i, j0t=j0t, nj=nj, isdiag=isdiag,
                                 gci=gci, s_e=s_e))
            while pend:
                emit_mm2(pend.pop(0))

            if sub3 < 4:
                continue
            qs_prev_ctx_dma = (("qAa" if (h - 1) % 2 == 0 else "qAb"),
                               16 * ((h - 1) // 2 + 1)) if h >= 1 else None
            NG = (NT + 7) // 8
            for g in range(NG):
                n8 = min(8, NT - g * 8)
                def ctr(t, h=h, g=g, n8=n8, vct=P.val("vCtxT")):
                    t.wait_ge(sems["vCtx"], (h + 1) * NT)
                    t.wait_ge(sems["vCtxT"], vct)
                    for q8 in range(n8):
                        ti = g * 8 + q8
                        inst = t.transpose(ps_tr[:, q8 * 128:(q8 + 1) * 128],
                                           ctx_nat[:, ti * 128:(ti + 1) * 128], idn_bf[:])
                    inst.then_inc(sems["tTRC"], 1)
                P.inc("tTRC", 1)
                P.emit("tensor", ctr)
                t_trc = P.val("tTRC")

                def ctev(v, g=g, n8=n8, tt=t_trc, qs=qs_prev_ctx_dma):
                    v.wait_ge(sems["tTRC"], tt)
                    if g == 0 and qs is not None:
                        v.wait_ge(sems[qs[0]], qs[1])
                    v.tensor_copy(ctxT_b[:, g * 1024:g * 1024 + n8 * 128],
                                  ps_tr[:, 0:n8 * 128]).then_inc(sems["vCtxT"], 1)
                P.inc("vCtxT", 1)
                P.emit("vector", ctev)
            dma("sync", "qAa" if h % 2 == 0 else "qAb",
                ag_in.ap()[h * 128:(h + 1) * 128, :], ctxT_b[:],
                waits=[("vCtxT", P.val("vCtxT"))])

        if debug:
            DBG_Q = nc.dram_tensor("DBG_Q", [128, NT * 128], BF16, kind="ExternalOutput").ap()
            DBG_K = nc.dram_tensor("DBG_K", [128, NT * 128], BF16, kind="ExternalOutput").ap()
            DBG_V = nc.dram_tensor("DBG_V", [128, NT * 129], BF16, kind="ExternalOutput").ap()
            DBG_CN = nc.dram_tensor("DBG_CN", [128, NT * 128], BF16, kind="ExternalOutput").ap()
            dma("sync", "qS", DBG_Q, qbuf[0][:], waits=[("vCtxT", P.val("vCtxT"))])
            dma("sync", "qS", DBG_K, kbuf[0][:], waits=[])
            dma("sync", "qS", DBG_V, vbuf[0][:], waits=[])
            dma("sync", "qS", DBG_CN, ctx_nat[:], waits=[])

        def coll1(g):
            g.wait_ge(sems["qAa"], 16 * ((HL - 1 + 1) // 2))
            g.wait_ge(sems["qAb"], 16 * ((HL - 1) // 2))
            g.collective_compute(
                "AllGather", ALU.bypass,
                replica_groups=[list(range(n_ag_cores))],
                ins=[ag_in.ap()[0:R1, :].opt()],
                outs=[ag_out1.ap().opt()],
            ).then_inc(sems["cc"], 1)
        P.inc("cc", 1)
        P.emit("gpsimd", coll1)

        qs_all_ctx = P.val("qS")
        barrier()

        if phases < 5:
            P.emit("gpsimd", lambda e, f=P.val("qG"): e.wait_ge(sems["qG"], f))
            return _finish(nc, P, sems)
        wfn_t = WFN.rearrange("(m p) c -> p m c", p=128)
        for m in range(MH):
            w = [("vTERN", P.val("vTERN") - 1)] if m >= 2 else []
            qW = "qWa" if m % 2 == 0 else "qWb"
            dma("sync", qW, wfn[m % 2][:].unsqueeze(1), wfn_t[:, m:m + 1, :], waits=w)
            qs_w = (qW, 16 * (m // 2 + 1))

            def sabs(s, m=m, qs=qs_w, vt=P.val("vTERN")):
                s.wait_ge(sems[qs[0]], qs[1])
                if m >= 2:
                    s.wait_ge(sems["vTERN"], vt)
                s.activation(absn[m % 2][:], wfn[m % 2][:], AF.Abs).then_inc(
                    sems["sAbs"], 1)
            P.inc("sAbs", 1)
            P.emit("scalar", sabs)
            s_a = P.val("sAbs")

            vst = P.inc("vSelf", 7)
            def tern_m(v, m=m, sa=s_a, vs=vst):
                I = sems["vSelf"]
                v.wait_ge(sems["sAbs"], sa)
                v.reduce_sum(redcol[:], absn[m % 2][:],
                             axis=mybir.AxisListType.X).then_inc(I, 1)
                v.wait_ge(I, vs - 6)
                v.tensor_single_scalar(thr_col[:, m:m + 1], redcol[:], 0.7 / D,
                                       ALU.mult).then_inc(I, 1)
                v.wait_ge(I, vs - 5)
                v.tensor_single_scalar(thr_col[:, MH + m:MH + m + 1],
                                       thr_col[:, m:m + 1], -1.0,
                                       ALU.mult).then_inc(I, 1)
                v.wait_ge(I, vs - 4)
                v.tensor_scalar(m1b[:], wfn[m % 2][:], thr_col[:, MH + m:MH + m + 1],
                                None, ALU.add).then_inc(I, 1)
                v.tensor_scalar(m2b[:], wfn[m % 2][:], thr_col[:, m:m + 1],
                                None, ALU.add).then_inc(I, 1)
                v.wait_ge(I, vs - 2)
                v.tensor_single_scalar(m1b[:], m1b[:], 0.0, ALU.is_gt).then_inc(I, 1)
                v.tensor_single_scalar(m2b[:], m2b[:], 0.0, ALU.is_lt).then_inc(I, 1)
                v.wait_ge(I, vs)
                v.tensor_tensor(qnat[m % 2][:], m1b[:], m2b[:],
                                ALU.subtract).then_inc(sems["vTERN"], 1)
            P.inc("vTERN", 1)
            P.emit("vector", tern_m)
            v_q = P.val("vTERN")

            NG8 = (KD + 7) // 8
            for g8 in range(NG8):
                n8 = min(8, KD - g8 * 8)
                def qtr(t, m=m, g8=g8, n8=n8, vq=v_q, vct=P.val("vCtxT")):
                    t.wait_ge(sems["vTERN"], vq)
                    t.wait_ge(sems["vCtxT"], vct)
                    for q8 in range(n8):
                        kk = g8 * 8 + q8
                        inst = t.transpose(ps_tr[:, q8 * 128:(q8 + 1) * 128],
                                           qnat[m % 2][:, kk * 128:(kk + 1) * 128],
                                           idn_bf[:])
                    inst.then_inc(sems["tTHR"], 1)
                P.inc("tTHR", 1)
                P.emit("tensor", qtr)
                t_q = P.val("tTHR")

                def qev(v, m=m, g8=g8, n8=n8, tq=t_q):
                    v.wait_ge(sems["tTHR"], tq)
                    dst = qt_sb[:, 0:KD * DH2].rearrange("p (k c) -> p k c", c=DH2) \
                        [:, g8 * 8:g8 * 8 + n8, m * 128:(m + 1) * 128]
                    src_ = ps_tr[:, 0:n8 * 128].rearrange("p (n c) -> p n c", c=128)
                    v.tensor_copy(dst, src_).then_inc(sems["vCtxT"], 1)
                P.inc("vCtxT", 1)
                P.emit("vector", qev)
        v_tern_done = P.val("vCtxT")

        wo_all = WO.rearrange("m (k p) c -> p (m k) c", p=128)
        for m in range(MH):
            dma("gpsimd", "qWO", wo_sb[:, m * KD * 128:(m + 1) * KD * 128]
                .rearrange("p (k c) -> p k c", c=128),
                wo_all[:, m * KD:(m + 1) * KD, :])
        qg_wo = P.val("qWO")

        if phases < 4:
            P.emit("sync", lambda e, f=P.val("qS"): e.wait_ge(sems["qS"], f))
            return _finish(nc, P, sems)
        def coll2(g):
            g.collective_compute(
                "AllGather", ALU.bypass,
                replica_groups=[list(range(n_ag_cores))],
                ins=[ag_in.ap()[R1:HL * 128, :].opt()],
                outs=[ag_out2.ap().opt()],
            ).then_inc(sems["cc2"], 1)
        P.inc("cc2", 1)
        P.emit("gpsimd", coll2)

        ctxF_t = ctxF.ap().rearrange("(k p) t -> p k t", p=128)
        P.queue_sems["gpsimd"] |= {"qGXa", "qGXb", "qSGa", "qSGb"}
        ks = [k for k in range(KD) if k % HL != HL - 1] + \
             [k for k in range(KD) if k % HL == HL - 1]
        for ki, k in enumerate(ks):
            qGX = "qGXa" if ki % 2 == 0 else "qGXb"
            qSG = "qSGa" if ki % 2 == 0 else "qSGb"
            src = ag_out1 if k % HL != HL - 1 else ag_out2
            ccs = "cc" if k % HL != HL - 1 else "cc2"
            ws = [(qSG, 16 * (ki // 2))] if ki >= 2 else []
            P.inc(qGX, 16)
            def gat(g, k=k, ki=ki, ws=tuple(ws), qGX=qGX, src=src, ccs=ccs):
                g.wait_ge(sems[ccs], 1)
                for (s, v) in ws:
                    g.wait_ge(sems[s], v)
                g.indirect_dma_start(
                    gbuf[ki % 2][:], None,
                    src.ap(), IndirectOffsetOnAxis(cidx_sb[:, k:k + 1], 0),
                ).then_inc(sems[qGX], 16)
            P.emit("gpsimd", gat)
            g_done = (qGX, 16 * (ki // 2 + 1))
            P.inc(qSG, 16)
            def stg(g, k=k, ki=ki, gd=g_done, qSG=qSG):
                g.wait_ge(sems[gd[0]], gd[1])
                g.dma_start(out=ctxF_t[:, k:k + 1, :],
                            in_=gbuf[ki % 2][:].unsqueeze(1)).then_inc(sems[qSG], 16)
            P.emit("gpsimd", stg)


        barrier()

        if phases < 6:
            P.emit("sync", lambda e, f=P.val("qS"): e.wait_ge(sems["qS"], f))
            P.emit("gpsimd", lambda e, f=P.val("qG"): e.wait_ge(sems["qG"], f))
            return _finish(nc, P, sems)
        xres_t = XRES.rearrange("(m p) t -> p m t", p=128)
        n_out = 0
        for tc in range(TC):
            w = [("tOUT", (tc - 1) * MH)] if tc >= 2 else []
            q3 = "q3a" if tc % 2 == 0 else "q3b"
            dma("sync", q3, p3_xn[tc % 2][:].rearrange("p (k t) -> p k t", t=512),
                xno_t[:, :, tc * 512:(tc + 1) * 512], waits=w)
            dma("sync", q3, p3_ctx[tc % 2][:].rearrange("p (k t) -> p k t", t=512),
                ctxF_t[:, :, tc * 512:(tc + 1) * 512])
            wv_ = [("vOut", (tc - 1) * MH)] if tc >= 2 else []
            dma("sync", q3, p3_xr[tc % 2][:].rearrange("p (m t) -> p m t", t=512),
                xres_t[:, :, tc * 512:(tc + 1) * 512], waits=wv_)
            qs_slabs = (q3, 48 * (tc // 2 + 1))

            for m in range(MH):
                ps_o = ps_q if m % 2 == 0 else ps_k
                n_out += 1

                def p3mm(t, tc=tc, m=m, qs=qs_slabs, ps_o=ps_o, n=n_out):
                    t.wait_ge(sems[qs[0]], qs[1])
                    if n == 1:
                        t.wait_ge(sems["vCtxT"], v_tern_done)
                        t.wait_ge(sems["qWO"], qg_wo)
                    if n >= 3:
                        t.wait_ge(sems["vOut"], n - 2)
                    for k in range(KD):
                        t.matmul(ps_o[:, 0:512],
                                 qt_sb[:, k * DH2 + m * 128:k * DH2 + (m + 1) * 128],
                                 p3_xn[tc % 2][:, k * 512:(k + 1) * 512],
                                 start=(k == 0), stop=False)
                    for k in range(KD):
                        inst = t.matmul(
                            ps_o[:, 0:512],
                            wo_sb[:, (m * KD + k) * 128:(m * KD + k + 1) * 128],
                            p3_ctx[tc % 2][:, k * 512:(k + 1) * 512],
                            start=False, stop=(k == KD - 1))
                        if k == KD - 1:
                            inst.then_inc(sems["tOUT"], 1)
                P.inc("tOUT", 1)
                P.emit("tensor", p3mm)
                t_o = P.val("tOUT")

                qOp = "qOa" if m % 2 == 0 else "qOb"
                st_guard = (qOp, P.val(qOp)) if P.val(qOp) > 0 else None

                def p3ev(v, tc=tc, m=m, to=t_o, ps_o=ps_o, sg=st_guard):
                    v.wait_ge(sems["tOUT"], to)
                    if sg is not None:
                        v.wait_ge(sems[sg[0]], sg[1])
                    v.scalar_tensor_tensor(out_st[m % 2][:], ps_o[:, 0:512],
                                           bias_sum[:, m:m + 1],
                                           p3_xr[tc % 2][:, m * 512:(m + 1) * 512],
                                           ALU.add, ALU.add).then_inc(sems["vOut"], 1)
                P.inc("vOut", 1)
                P.emit("vector", p3ev)
                v_o = P.val("vOut")

                P.queue_sems["sync"].add(qOp)
                P.inc(qOp, 16)
                def p3st(e, tc=tc, m=m, vo=v_o, qOp=qOp):
                    e.wait_ge(sems["vOut"], vo)
                    e.dma_start(out=OUT[m * 128:(m + 1) * 128, tc * 512:(tc + 1) * 512],
                                in_=out_st[m % 2][:]).then_inc(sems[qOp], 16)
                P.emit("sync", p3st)

        if debug:
            DBG_XNO = nc.dram_tensor("DBG_XNO", [D, S], BF16, kind="ExternalOutput").ap()
            DBG_AGI = nc.dram_tensor("DBG_AGI", [HL * 128, S], BF16, kind="ExternalOutput").ap()
            DBG_CF = nc.dram_tensor("DBG_CF", [D, S], BF16, kind="ExternalOutput").ap()
            DBG_QT = nc.dram_tensor("DBG_QT", [128, KD * DH2], BF16, kind="ExternalOutput").ap()
            DBG_THRB = nc.dram_tensor("DBG_THRB", [128, 2 * MH], F32, kind="ExternalOutput").ap()
            dma("sync", "qS", DBG_QT, qt_sb[:], waits=[("qO", P.val("qO"))])
            dma("sync", "qS", DBG_THRB, thr_col[:], waits=[])
            DBG_T1 = nc.dram_tensor("DBG_T1", [128, CHKT * 128], F32, kind="ExternalOutput").ap()
            DBG_CALL = nc.dram_tensor("DBG_CALL", [128, HL * NT], F32, kind="ExternalOutput").ap()
            DBG_SLB = nc.dram_tensor("DBG_SLB", [128, HL], F32, kind="ExternalOutput").ap()
            DBG_JP = nc.dram_tensor("DBG_JP", [128, CHKT * 128], F32, kind="ExternalOutput").ap()
            dma("sync", "qS", DBG_T1, t1[:], waits=[("qO", P.val("qO"))])
            dma("sync", "qS", DBG_CALL, c_all[:], waits=[])
            dma("sync", "qS", DBG_SLB, slope_bc[:], waits=[])
            dma("sync", "qS", DBG_JP, jp[:], waits=[])
            dma("sync", "qS", DBG_XNO, xnO.ap(), waits=[])
            dma("sync", "qS", DBG_AGI, ag_in.ap(), waits=[])
            dma("sync", "qS", DBG_CF, ctxF.ap(), waits=[])

        drain_s = tuple((s, P.val(s))
                        for s in sorted(P.queue_sems["sync"] | {"qS", "qO"}))
        out_tot = tuple((s, P.val(s)) for s in ("qOa", "qOb"))

        def fin_s(e, ws=drain_s):
            for (s, v) in ws:
                e.wait_ge(sems[s], v)
        P.emit("sync", fin_s)

        def fin_g(e, ws=out_tot):
            for (s, v) in ws:
                e.wait_ge(sems[s], v)
        P.emit("gpsimd", fin_g)

        return _finish(nc, P, sems)


def _finish(nc, P, sems):
    with nc.Block() as block:
        @block.sync
        def _(eng):
            for fn in P.q["sync"]:
                fn(eng)

        @block.gpsimd
        def _(eng):
            for fn in P.q["gpsimd"]:
                fn(eng)

        @block.tensor
        def _(eng):
            for fn in P.q["tensor"]:
                fn(eng)

        @block.vector
        def _(eng):
            for fn in P.q["vector"]:
                fn(eng)

        @block.scalar
        def _(eng):
            for fn in P.q["scalar"]:
                fn(eng)

    return nc


def make_consts(S):
    NT = S // 128
    g = np.arange(CHKT * 128)
    p = np.arange(128)
    JPC = (128 * (g[None, :] // 128) + p[:, None] - (g[None, :] % 128)).astype(np.float32)
    IVEC = (-128.0 * np.arange(NT, dtype=np.float32)).reshape(1, NT)
    TRIU = (np.arange(128)[None, :] >= np.arange(128)[:, None]).astype(np.float32)
    MSKD = np.where(np.arange(128)[:, None] > np.arange(128)[None, :],
                    np.float32(-1e6), np.float32(0.0)).astype(np.float32)
    IDN = np.eye(128, dtype=np.float32)
    return dict(JPC=JPC, IVEC=IVEC, TRIU=TRIU, MSKD=MSKD, IDN=IDN,
                ONESR=np.ones((1, 128), np.float32),
                ONESC=np.ones((128, 1), np.float32),
                ONESQ=np.ones((128, 128), np.float32),
                ONES2=np.ones((2, 128), np.float32),
                ONE1=np.ones((1, 1), np.float32))


def make_core_inputs(c, x, Wq, bq, Wk, bk, Wv, bv, Wo, bo, slopes, Wf, bf, S, D, HL):
    DH2 = D // 2
    MH = DH2 // 128
    KD = D // 128
    b = c // 2
    hg = c % 2
    hrows = slice(hg * HL * 128, (hg + 1) * HL * 128)
    dmrows = slice(hg * DH2, (hg + 1) * DH2)

    d = {}
    d["XT"] = np.ascontiguousarray(x[b].T)
    for nm, W in (("WQ", Wq), ("WK", Wk), ("WV", Wv)):
        d[nm] = np.ascontiguousarray(W[hrows].reshape(HL, 128, D).transpose(0, 2, 1))
    d["BQ"] = np.ascontiguousarray(bq[hrows].reshape(1, -1))
    d["BK"] = np.ascontiguousarray(bk[hrows].reshape(1, -1))
    d["BV"] = np.ascontiguousarray(bv[hrows].reshape(1, -1))
    d["SLP"] = np.ascontiguousarray(slopes[hg * HL:(hg + 1) * HL].reshape(1, HL))
    d["WO"] = np.ascontiguousarray(Wo[dmrows].reshape(MH, 128, D).transpose(0, 2, 1))
    d["BO"] = np.ascontiguousarray(bo[dmrows].reshape(1, -1))
    d["WFN"] = np.ascontiguousarray(Wf[dmrows])
    d["BF2"] = np.ascontiguousarray(bf[dmrows].reshape(1, -1))
    d["XRES"] = np.ascontiguousarray(x[b].T[dmrows, :])
    pidx = np.arange(128)[:, None]
    cidx = np.empty((128, KD), np.int64)
    for kk in range(KD):
        src = 2 * b + kk // HL
        hk = kk % HL
        if hk < HL - 1:
            cidx[:, kk:kk + 1] = src * (HL - 1) * 128 + hk * 128 + pidx
        else:
            cidx[:, kk:kk + 1] = src * 128 + pidx
    d["CIDX"] = np.ascontiguousarray(cidx.astype(np.int32))
    return d


def kernel(L0_attn_hidden_states, L0_attn_Wq, L0_attn_bq, L0_attn_Wk, L0_attn_bk,
           L0_attn_Wv, L0_attn_bv, L0_attn_Wo, L0_attn_bo, L0_attn_alibi_slopes,
           L0_ffn_weight, L0_ffn_bias):
    S, D, HL = S_FULL, D_FULL, H_FULL // 2
    f = lambda a: np.asarray(a, np.float32)
    x = f(L0_attn_hidden_states)
    consts = make_consts(S)
    in_maps = []
    for c in range(NCORES):
        d = make_core_inputs(c, x, f(L0_attn_Wq), f(L0_attn_bq), f(L0_attn_Wk),
                             f(L0_attn_bk), f(L0_attn_Wv), f(L0_attn_bv),
                             f(L0_attn_Wo), f(L0_attn_bo), f(L0_attn_alibi_slopes),
                             f(L0_ffn_weight), f(L0_ffn_bias), S, D, HL)
        d.update(consts)
        in_maps.append(d)

    nc = build_nc(S, D, HL, NCORES)
    res = run_bass_kernel_spmd(nc, in_maps, core_ids=list(range(NCORES)))

    out = np.empty((B_FULL, S, D), np.float32)
    for b in range(B_FULL):
        top = np.asarray(res.results[2 * b]["OUT"], np.float32)
        bot = np.asarray(res.results[2 * b + 1]["OUT"], np.float32)
        out[b] = np.concatenate([top, bot], axis=0).T
    return out
